# version 16
# baseline (speedup 1.0000x reference)
import numpy as np
import concourse.bass as bass
import concourse.mybir as mybir
from concourse.bass_utils import run_bass_kernel_spmd

F32 = mybir.dt.float32
BF16 = mybir.dt.bfloat16
AF = mybir.ActivationFunctionType
ALU = mybir.AluOpType
AX = mybir.AxisListType.X

D = 1024
NG = 8
NE = 8
TWO_PI = float(2.0 * np.pi)
RND_MAGIC = 12582912.0

COMPUTE = ("pe", "dve", "act", "pool")
N_HD = 16
N_GD = 8


class Prog:

    def __init__(self, nc):
        self.nc = nc
        self.sem = {c: nc.alloc_semaphore("sem_" + c) for c in COMPUTE}
        for i in range(N_HD):
            self.sem[("hd", i)] = nc.alloc_semaphore(f"sem_hd{i}")
        for i in range(N_GD):
            self.sem[("gd", i)] = nc.alloc_semaphore(f"sem_gd{i}")
        self.count = {c: 0 for c in self.sem}
        self.meta = []
        self.cur = None
        self.rr = {"hd": 0, "gd": 0}

    def _emit(self, chan, eng, fn, deps, inc):
        assert self.cur is not None, "emit outside phase"
        waits = []
        lw = self.cur["lw"][eng]
        for d in deps:
            if d is None:
                continue
            dchan, dcount = self.meta[d]
            if lw.get(dchan, 0) < dcount:
                waits.append((dchan, dcount))
                lw[dchan] = dcount
        self.count[chan] += inc
        self.cur["ops"][eng].append((fn, chan, inc, waits))
        self.meta.append((chan, self.count[chan]))
        return len(self.meta) - 1

    def pe(self, fn, deps=()):
        return self._emit("pe", "pe", fn, deps, 1)

    def dve(self, fn, deps=()):
        return self._emit("dve", "dve", fn, deps, 1)

    def act(self, fn, deps=()):
        return self._emit("act", "act", fn, deps, 1)

    def pool(self, fn, deps=()):
        return self._emit("pool", "pool", fn, deps, 1)

    def _dma(self, pool, eng, fn, deps):
        i = self.rr[pool]
        self.rr[pool] = (i + 1) % (N_HD if pool == "hd" else N_GD)
        chan = (pool, i)
        lw = self.cur["lw"][eng]
        waits = []
        prev = self.count[chan]
        if prev and lw.get(chan, 0) < prev:
            waits.append((chan, prev))
            lw[chan] = prev
        for d in deps:
            if d is None:
                continue
            dchan, dcount = self.meta[d]
            if lw.get(dchan, 0) < dcount:
                waits.append((dchan, dcount))
                lw[dchan] = dcount
        self.count[chan] += 16
        self.cur["ops"][eng].append((fn, chan, 16, waits))
        self.meta.append((chan, self.count[chan]))
        return len(self.meta) - 1

    def dma(self, fn, deps=()):
        return self._dma("hd", "sp", fn, deps)

    def dmag(self, fn, deps=()):
        return self._dma("gd", "pool", fn, deps)

    def phase(self):
        old = self.cur
        lw = {}
        for eng in ("pe", "dve", "act", "pool", "sp"):
            lw[eng] = {c: self.count[c] for c in COMPUTE}
            if old is not None:
                for c, v in old["lw"][eng].items():
                    if c not in COMPUTE:
                        lw[eng][c] = v
        self.cur = {"ops": {e: [] for e in ("pe", "dve", "act", "pool", "sp")}, "lw": lw}

    def close(self, name=None):
        nc, sem = self.nc, self.sem
        ops = self.cur["ops"]
        with nc.Block(name) as blk:
            def mk(engname):
                def section(engine):
                    for fn, chan, inc, waits in ops[engname]:
                        for wchan, wcount in waits:
                            engine.wait_ge(sem[wchan], wcount)
                        fn(engine).then_inc(sem[chan], inc)
                return section
            if ops["sp"]:
                blk.sync(mk("sp"))
            if ops["pe"]:
                blk.tensor(mk("pe"))
            if ops["dve"]:
                blk.vector(mk("dve"))
            if ops["act"]:
                blk.scalar(mk("act"))
            if ops["pool"]:
                blk.gpsimd(mk("pool"))
        self.phase()

    def final_wait(self, deps):
        s = self
        with self.nc.Block("final") as blk:
            def section(engine):
                for d in deps:
                    dchan, dcount = s.meta[d]
                    engine.wait_ge(s.sem[dchan], dcount)
            blk.sync(section)


def build(nc, L=4096, C=1024, debug=()):
    NCH = L // C
    NTC = L // 512

    dt = {}
    def din(name, shape, dtype=F32):
        dt[name] = nc.dram_tensor(name, list(shape), dtype, kind="ExternalInput")
        return dt[name]

    din("xt", (D, L))
    din("pp", (128, 64))
    din("nrep", (128, 4 * D))
    din("wq", (NG, D, 128)); din("wk", (NG, D, 128)); din("wv", (NG, D, 128))
    din("wgate", (D, D)); din("wout", (D, D))
    din("swp", (NG, 128, 2))
    din("mgw", (D, NE)); din("mgb_rep", (128, NE)); din("rsc_rep", (128, NE))
    din("w1", (NE, D, 2048)); din("w2", (NE, 2048, D))
    din("b1p", (NE, 128, 16)); din("lgp", (NE, 128, 16)); din("lbp", (NE, 128, 16))
    din("b2", (NE, D))
    din("identb", (128, 128), BF16)
    din("identf", (128, 128))
    din("iota8", (128, NE))
    din("onesc_b", (128, 1), BF16)
    din("onesr_f", (1, 128))
    din("bc2", (2, 128))
    din("swones_b", (128, 2), BF16)
    din("eoh", (NE, NE * 128), BF16)

    out = nc.dram_tensor("out", [L, D], F32, kind="ExternalOutput")
    yacc = nc.dram_tensor("yacc", [L, D], F32, kind="Internal")
    x1tok = nc.dram_tensor("x1tok", [L, D], BF16, kind="Internal")
    dbg = {}
    for name in debug:
        shp = {"dz": (D, L), "dx1": (D, L), "dx2": (D, L), "dmwr": (8, L), "doa": (D, L), "dtk": (128, 16 * (L // 128)), "dlg": (128, 8 * (L // 128)), "dxf": (128, 1024), "dps": (128, 8)}[name]
        dbg[name] = nc.dram_tensor(name, list(shp), F32, kind="ExternalOutput")

    SB_BYTES = 205 * 1024
    big = nc.alloc_sbuf_tensor("big", [128, SB_BYTES], mybir.dt.uint8, side="left")

    def sb(name, shape, dtype=F32, at=None):
        nbytes = int(np.prod(shape[1:])) * mybir.dt.size(dtype)
        assert at is not None and at % 4 == 0
        assert at + nbytes <= SB_BYTES, f"SBUF overflow: {name} ends {at + nbytes}"
        view = big[0:shape[0], at:at + nbytes].bitcast(dtype)
        assert tuple(view.shape) == tuple(shape), (name, view.shape, shape)
        return view

    o = 0
    def nxt(n):
        nonlocal o
        r = o
        o = r + ((n + 31) // 32) * 32
        return r

    pp = sb("pp_sb", (128, 64), at=nxt(256))
    identb = sb("identb_sb", (128, 128), BF16, at=nxt(256))
    identf = sb("identf_sb", (128, 128), at=nxt(512))
    iota8 = sb("iota8_sb", (128, NE), at=nxt(32))
    onesc_b = sb("onesc_sb", (128, 1), BF16, at=nxt(32))
    onesr_f = sb("onesr_sb", (1, 128), at=nxt(512))
    bc2 = sb("bc2_sb", (2, 128), at=nxt(512))
    swones_b = sb("swones_sb", (128, 2), BF16, at=nxt(32))
    eoh = sb("eoh_sb", (NE, NE * 128), BF16, at=nxt(2048))
    mgb = sb("mgb_sb", (128, NE), at=nxt(32))
    rscr = sb("rsc_sb", (128, NE), at=nxt(32))
    dparams = sb("dparams", (128, 128), at=nxt(512))
    pscr = sb("pscr", (128, 256), at=nxt(1024))
    ek = sb("ek", (128, 2 * 12), at=nxt(96))
    kc = sb("kc", (128, 8), at=nxt(32))
    cvals = sb("cvals", (128, 4), at=nxt(32))
    PERSIST_END = o
    assert PERSIST_END <= 8 * 1024, PERSIST_END

    Z_OFF = 16 * 1024
    z = sb("z", (128, NG * L), F32, at=Z_OFF)
    x1b = [sb(f"x1b{g}", (128, L), BF16, at=Z_OFF + g * L * 4) for g in range(NG)]
    SCRATCH = Z_OFF + NG * L * 4
    ssm_names = ["Cc", "Cs", "aux", "aux2", "cct", "sst", "gre", "gim", "rhot"]
    ssm = {}
    for i, nm in enumerate(ssm_names):
        ssm[nm] = sb(nm, (128, C), F32, at=SCRATCH + i * C * 4)
    zbch = sb("zbch", (128, 512), BF16, at=SCRATCH)
    sqch = sb("sqch", (128, 512), BF16, at=SCRATCH + 1024)
    tsc = sb("tsc", (128, 512), F32, at=SCRATCH + 2048)
    r0 = sb("r0", (1, 512), at=SCRATCH + 4096)
    r1 = sb("r1", (1, 512), at=SCRATCH + 6144)
    r2 = sb("r2", (1, 512), at=SCRATCH + 8192)
    nbc = sb("nbc", (128, 4 * D), at=SCRATCH + 10240)

    pbig = nc.alloc_psum_tensor("pbig", [128, 2048], F32)
    pb4 = nc.alloc_psum_tensor("pb4", [128, 512], F32)
    pb5 = nc.alloc_psum_tensor("pb5", [128, 512], F32)
    pb7 = nc.alloc_psum_tensor("pb7", [128, 512], F32)
    pbt = nc.alloc_psum_tensor("pbt", [128, 1024], BF16)

    pr = Prog(nc)
    pr.phase()

    ld = {}
    for nm, t in [("pp", pp), ("identb", identb), ("identf", identf), ("iota8", iota8),
                  ("onesc_b", onesc_b), ("onesr_f", onesr_f), ("bc2", bc2),
                  ("swones_b", swones_b), ("eoh", eoh)]:
        ld[nm] = pr.dma(lambda e, nm=nm, t=t: e.dma_start(t[:], dt[nm][:]))
    ld["mgb"] = pr.dma(lambda e: e.dma_start(mgb[:], dt["mgb_rep"][:]))
    ld["rsc"] = pr.dma(lambda e: e.dma_start(rscr[:], dt["rsc_rep"][:]))
    ldx = [pr.dma(lambda e, g=g: e.dma_start(z[:, g * L:(g + 1) * L],
                                             dt["xt"][g * 128:(g + 1) * 128, :]))
           for g in range(NG)]

    P_INIT_RE, P_INIT_IM, P_ANGLE, P_LNM, P_LC_RE, P_LC_IM, P_LPS, P_GATEB = range(8)
    RHO, GATE, COSTH, SINTH, GI_RE, GI_IM, GIN_RE, GIN_IMN = range(8)
    GI_REN = 8
    ppc = lambda i: pp[:, i * 8:(i + 1) * 8]
    dpc = lambda i: dparams[:, i * 8:(i + 1) * 8]
    scr = lambda i: pscr[:, i * 8:(i + 1) * 8]

    c_eps = pr.dve(lambda e: e.memset(cvals[:, 0:1], 1e-5))
    dep = ld["pp"]
    o1 = pr.act(lambda e: e.activation(scr(0), ppc(P_LNM), AF.Exp), deps=[dep])
    o2 = pr.dve(lambda e: e.tensor_tensor(scr(1), scr(0), ppc(P_LPS), ALU.mult), deps=[o1])
    o3 = pr.act(lambda e: e.activation(dpc(RHO), scr(1), AF.Exp, scale=-1.0), deps=[o2])
    o4 = pr.dve(lambda e: e.tensor_scalar(dpc(GATE), dpc(RHO), -1.0, 1.0, ALU.mult, ALU.add), deps=[o3])
    t1 = pr.dve(lambda e: e.tensor_tensor(scr(2), ppc(P_ANGLE), ppc(P_LPS), ALU.mult), deps=[dep])
    t2 = pr.dve(lambda e: e.tensor_scalar(scr(3), scr(2), 1.0 / TWO_PI, RND_MAGIC, ALU.mult, ALU.add), deps=[t1])
    t3 = pr.dve(lambda e: e.tensor_scalar(scr(4), scr(3), RND_MAGIC, -TWO_PI, ALU.subtract, ALU.mult), deps=[t2])
    t4 = pr.dve(lambda e: e.tensor_tensor(scr(5), scr(2), scr(4), ALU.add), deps=[t3])
    u1 = pr.act(lambda e: e.activation(scr(6), scr(5), AF.Sin, scale=0.5), deps=[t4])
    u2 = pr.act(lambda e: e.activation(scr(7), scr(6), AF.Square), deps=[u1])
    u3 = pr.dve(lambda e: e.tensor_scalar(dpc(COSTH), scr(7), -2.0, 1.0, ALU.mult, ALU.add), deps=[u2])
    u4 = pr.act(lambda e: e.activation(scr(8), scr(5), AF.Sin, scale=0.25), deps=[t4])
    u5 = pr.act(lambda e: e.activation(scr(9), scr(8), AF.Square), deps=[u4])
    u6 = pr.dve(lambda e: e.tensor_scalar(scr(10), scr(9), -2.0, 1.0, ALU.mult, ALU.add), deps=[u5])
    u7 = pr.dve(lambda e: e.scalar_tensor_tensor(dpc(SINTH), scr(6), 2.0, scr(10), ALU.mult, ALU.mult),
                deps=[u6, u1])
    g1 = pr.dve(lambda e: e.tensor_tensor(dpc(GI_RE), dpc(GATE), ppc(P_INIT_RE), ALU.mult), deps=[o4])
    g2 = pr.dve(lambda e: e.tensor_tensor(dpc(GI_IM), dpc(GATE), ppc(P_INIT_IM), ALU.mult), deps=[o4])
    g3 = pr.dve(lambda e: e.tensor_scalar(dpc(GI_REN), dpc(GI_RE), -1.0, None, ALU.mult), deps=[g1])
    q1 = pr.act(lambda e: e.activation(scr(11), ppc(P_INIT_RE), AF.Square), deps=[dep])
    q2 = pr.act(lambda e: e.activation(scr(12), ppc(P_INIT_IM), AF.Square), deps=[dep])
    q3 = pr.dve(lambda e: e.tensor_tensor(scr(13), scr(11), scr(12), ALU.add), deps=[q1, q2])
    q4 = pr.dve(lambda e: e.reciprocal(scr(14), scr(13)), deps=[q3])
    q5 = pr.dve(lambda e: e.tensor_tensor(scr(15), ppc(P_LC_RE), ppc(P_INIT_RE), ALU.mult), deps=[dep])
    q6 = pr.dve(lambda e: e.tensor_tensor(scr(16), ppc(P_LC_IM), ppc(P_INIT_IM), ALU.mult), deps=[dep])
    q7 = pr.dve(lambda e: e.tensor_tensor(scr(17), scr(15), scr(16), ALU.add), deps=[q5, q6])
    q8 = pr.dve(lambda e: e.tensor_tensor(scr(18), scr(17), scr(14), ALU.mult), deps=[q7, q4])
    q9 = pr.dve(lambda e: e.tensor_tensor(scr(19), ppc(P_LC_IM), ppc(P_INIT_RE), ALU.mult), deps=[dep])
    qa = pr.dve(lambda e: e.tensor_tensor(scr(20), ppc(P_LC_RE), ppc(P_INIT_IM), ALU.mult), deps=[dep])
    qb = pr.dve(lambda e: e.tensor_tensor(scr(21), scr(19), scr(20), ALU.subtract), deps=[q9, qa])
    qc = pr.dve(lambda e: e.tensor_tensor(scr(22), scr(21), scr(14), ALU.mult), deps=[qb, q4])
    r1_ = pr.dve(lambda e: e.tensor_tensor(scr(23), scr(18), dpc(COSTH), ALU.mult), deps=[q8, u3])
    r2_ = pr.dve(lambda e: e.tensor_tensor(scr(24), scr(22), dpc(SINTH), ALU.mult), deps=[qc, u7])
    r3_ = pr.dve(lambda e: e.tensor_tensor(dpc(GIN_RE), scr(23), scr(24), ALU.subtract), deps=[r1_, r2_])
    r4_ = pr.dve(lambda e: e.tensor_tensor(scr(25), scr(18), dpc(SINTH), ALU.mult), deps=[q8, u7])
    r5_ = pr.dve(lambda e: e.tensor_tensor(scr(26), scr(22), dpc(COSTH), ALU.mult), deps=[qc, u3])
    r6_ = pr.dve(lambda e: e.tensor_tensor(scr(27), scr(25), scr(26), ALU.add), deps=[r4_, r5_])
    r7_ = pr.dve(lambda e: e.tensor_scalar(dpc(GIN_IMN), scr(27), -1.0, None, ALU.mult), deps=[r6_])
    params_done = [o4, u3, u7, g1, g2, g3, r3_, r7_]
    pr.close("p0")

    Cc, Cs, aux, aux2, cct, sst, gre, gim, rhot = (ssm[n] for n in ssm_names)
    nlad = int(np.log2(C))
    z_done = []
    for g in range(NG):
        P = lambda view: view[:, g:g + 1]
        zg = z[:, g * L:(g + 1) * L]
        d0 = pr.dve(lambda e: e.memset(Cc[:, 0:1], 1.0), deps=params_done if g == 0 else z_done[-1:])
        d1 = pr.dve(lambda e: e.memset(Cs[:, 0:1], 0.0), deps=[d0])
        e0 = pr.dve(lambda e: e.tensor_copy(ek[:, 0:1], P(dpc(COSTH))), deps=[d1])
        e1 = pr.dve(lambda e: e.tensor_copy(ek[:, 1:2], P(dpc(SINTH))), deps=[e0])
        prev = e1
        for k in range(nlad):
            w = 1 << k
            ekre = ek[:, 2 * k:2 * k + 1]
            ekim = ek[:, 2 * k + 1:2 * k + 2]
            ng1 = pr.dve(lambda e, ekim=ekim: e.tensor_scalar(kc[:, 2:3], ekim, -1.0, None, ALU.mult),
                         deps=[prev])
            a1 = pr.dve(lambda e, w=w, ekre=ekre: e.tensor_scalar(aux[:, 0:w], Cc[:, 0:w], ekre, None,
                                                                  ALU.mult), deps=[ng1])
            a2 = pr.dve(lambda e, w=w: e.scalar_tensor_tensor(Cc[:, w:2 * w], Cs[:, 0:w], kc[:, 2:3],
                                                              aux[:, 0:w], ALU.mult, ALU.add), deps=[a1])
            a3 = pr.dve(lambda e, w=w, ekre=ekre: e.tensor_scalar(aux2[:, 0:w], Cs[:, 0:w], ekre, None,
                                                                  ALU.mult), deps=[a2])
            a4 = pr.dve(lambda e, w=w, ekim=ekim: e.scalar_tensor_tensor(Cs[:, w:2 * w], Cc[:, 0:w], ekim,
                                                                         aux2[:, 0:w], ALU.mult, ALU.add),
                        deps=[a3])
            b1 = pr.dve(lambda e, ekre=ekre: e.tensor_tensor(kc[:, 3:4], ekre, ekre, ALU.mult), deps=[a4])
            b2 = pr.dve(lambda e, ekim=ekim: e.tensor_tensor(kc[:, 4:5], ekim, ekim, ALU.mult), deps=[b1])
            b3 = pr.dve(lambda e, ekre=ekre, ekim=ekim: e.tensor_tensor(kc[:, 5:6], ekre, ekim, ALU.mult),
                        deps=[b2])
            b4 = pr.dve(lambda e, k=k: e.tensor_tensor(ek[:, 2 * k + 2:2 * k + 3], kc[:, 3:4], kc[:, 4:5],
                                                       ALU.subtract), deps=[b3])
            b5 = pr.dve(lambda e, k=k: e.tensor_scalar(ek[:, 2 * k + 3:2 * k + 4], kc[:, 5:6], 2.0, None,
                                                       ALU.mult), deps=[b4])
            prev = b5
        ECre = ek[:, 2 * nlad:2 * nlad + 1]
        ECim = ek[:, 2 * nlad + 1:2 * nlad + 2]
        rt = pr.dve(lambda e: e.tensor_scalar(rhot[:], Cc[:], 0.0, P(dpc(RHO)), ALU.mult, ALU.add),
                    deps=[prev])
        k0 = pr.dve(lambda e: e.memset(kc[:, 0:1], 1.0), deps=[rt])
        k1 = pr.dve(lambda e: e.memset(kc[:, 1:2], 0.0), deps=[k0])
        prev = k1
        carry_re, carry_im = P(dpc(GIN_RE)), P(dpc(GIN_IMN))
        for c in range(NCH):
            xgc = zg[:, c * C:(c + 1) * C]
            j0 = pr.dve(lambda e: e.tensor_scalar(kc[:, 2:3], kc[:, 1:2], -1.0, None, ALU.mult), deps=[prev])
            c1 = pr.act(lambda e: e.activation(aux[:], Cc[:], AF.Copy, scale=kc[:, 0:1]), deps=[j0])
            c2 = pr.dve(lambda e: e.scalar_tensor_tensor(cct[:], Cs[:], kc[:, 2:3], aux[:], ALU.mult,
                                                         ALU.add), deps=[c1])
            c3 = pr.act(lambda e: e.activation(aux2[:], Cs[:], AF.Copy, scale=kc[:, 0:1]), deps=[c2])
            c4 = pr.dve(lambda e: e.scalar_tensor_tensor(sst[:], Cc[:], kc[:, 1:2], aux2[:], ALU.mult,
                                                         ALU.add), deps=[c3])
            i1 = pr.dve(lambda e, xgc=xgc: e.tensor_tensor(aux[:], xgc, cct[:], ALU.mult),
                        deps=[c4, ldx[g]])
            i2 = pr.dve(lambda e, xgc=xgc: e.tensor_tensor(aux2[:], xgc, sst[:], ALU.mult), deps=[i1])
            sc1 = pr.dve(lambda e, cr=carry_re: e.tensor_tensor_scan(gre[:], rhot[:], aux[:], cr,
                                                                     ALU.mult, ALU.add), deps=[i2])
            sc2 = pr.dve(lambda e, ci=carry_im: e.tensor_tensor_scan(gim[:], rhot[:], aux2[:], ci,
                                                                     ALU.mult, ALU.add), deps=[sc1])
            p1 = pr.act(lambda e: e.activation(aux[:], gre[:], AF.Copy, scale=P(dpc(GI_RE))), deps=[sc2])
            p2 = pr.dve(lambda e: e.scalar_tensor_tensor(aux[:], gim[:], P(dpc(GI_IM)), aux[:], ALU.mult,
                                                         ALU.add), deps=[p1])
            p3 = pr.act(lambda e: e.activation(aux2[:], gre[:], AF.Copy, scale=P(dpc(GI_IM))), deps=[p2])
            p4 = pr.dve(lambda e: e.scalar_tensor_tensor(aux2[:], gim[:], P(dpc(GI_REN)), aux2[:], ALU.mult,
                                                         ALU.add), deps=[p3])
            p5 = pr.dve(lambda e: e.tensor_tensor(cct[:], aux[:], cct[:], ALU.mult), deps=[p4])
            p6 = pr.dve(lambda e: e.tensor_tensor(sst[:], aux2[:], sst[:], ALU.mult), deps=[p5])
            p7 = pr.dve(lambda e: e.tensor_tensor(cct[:], cct[:], sst[:], ALU.subtract), deps=[p6])
            p8 = pr.dve(lambda e, xgc=xgc: e.tensor_tensor(xgc, xgc, cct[:], ALU.add), deps=[p7])
            prev = p8
            carry_re = gre[:, C - 1:C]
            carry_im = gim[:, C - 1:C]
            if c < NCH - 1:
                k2 = pr.dve(lambda e: e.tensor_tensor(kc[:, 6:7], kc[:, 0:1], ECre, ALU.mult), deps=[prev])
                k3 = pr.dve(lambda e: e.tensor_tensor(kc[:, 7:8], kc[:, 1:2], ECim, ALU.mult), deps=[k2])
                k4 = pr.dve(lambda e: e.tensor_tensor(kc[:, 6:7], kc[:, 6:7], kc[:, 7:8], ALU.subtract),
                            deps=[k3])
                k5 = pr.dve(lambda e: e.tensor_tensor(kc[:, 7:8], kc[:, 0:1], ECim, ALU.mult), deps=[k4])
                k6 = pr.dve(lambda e: e.scalar_tensor_tensor(kc[:, 1:2], kc[:, 1:2], ECre, kc[:, 7:8],
                                                             ALU.mult, ALU.add), deps=[k5])
                k7 = pr.dve(lambda e: e.tensor_copy(kc[:, 0:1], kc[:, 6:7]), deps=[k6])
                prev = k7
        z_done.append(prev)
        pr.close(f"ssm_g{g}")

    z_readers = []
    if "dz" in dbg:
        z_readers = [pr.dma(lambda e, g=g: e.dma_start(dbg["dz"][g * 128:(g + 1) * 128, :],
                                                       z[:, g * L:(g + 1) * L]), deps=[z_done[g]])
                     for g in range(NG)]
        pr.close("dbg_z")

    x1_done = [None] * NG
    for tch in range(NTC):
        csl = slice(tch * 512, (tch + 1) * 512)
        mmm = None
        mms = None
        sq = None
        prev_cp = [z_done[-1]] if tch == 0 else [x1_done[-1]]
        for g in range(NG):
            cp = pr.dve(lambda e, g=g, tch=tch: e.tensor_copy(zbch[:], z[:, g * L + tch * 512:g * L + (tch + 1) * 512]),
                        deps=prev_cp + [mmm, sq])
            sq = pr.act(lambda e: e.activation(sqch[:], zbch[:], AF.Square),
                        deps=[cp, mms])
            mmm = pr.pe(lambda e, g=g: e.matmul(pb4[0:1, :], onesc_b[:], zbch[:],
                                                start=(g == 0), stop=(g == NG - 1)),
                        deps=[cp, ld["onesc_b"]])
            mms = pr.pe(lambda e, g=g: e.matmul(pb5[0:1, :], onesc_b[:], sqch[:],
                                                start=(g == 0), stop=(g == NG - 1)), deps=[sq])
            prev_cp = [mmm]
        v0 = pr.dve(lambda e: e.tensor_scalar(r0[:], pb4[0:1, :], 1.0 / D, None, ALU.mult), deps=[mmm])
        v1 = pr.dve(lambda e: e.tensor_scalar(r1[:], pb5[0:1, :], 1.0 / D, None, ALU.mult), deps=[mms, v0])
        v2 = pr.dve(lambda e: e.tensor_tensor(r2[:], r0[:], r0[:], ALU.mult), deps=[v1])
        v3 = pr.dve(lambda e: e.tensor_tensor(r1[:], r1[:], r2[:], ALU.subtract), deps=[v2])
        v4 = pr.act(lambda e: e.activation(r1[:], r1[:], AF.Ln, bias=cvals[0:1, 0:1]), deps=[v3])
        v5 = pr.act(lambda e: e.activation(r1[:], r1[:], AF.Exp, scale=-0.5), deps=[v4])
        bm = pr.pe(lambda e: e.matmul(pbig[:, 0:512], onesr_f[:], r0[:], start=True, stop=True),
                   deps=[v2, ld["onesr_f"]])
        br = pr.pe(lambda e: e.matmul(pbig[:, 512:1024], onesr_f[:], r1[:], start=True, stop=True), deps=[v5, bm])
        ap_prev = [br]
        for g in range(NG):
            s1 = pr.dve(lambda e, g=g, tch=tch: e.tensor_tensor(
                tsc[:], z[:, g * L + tch * 512:g * L + (tch + 1) * 512],
                pbig[:, 0:512], ALU.subtract), deps=ap_prev + [bm])
            s2 = pr.dve(lambda e, g=g, csl=csl: e.tensor_tensor(x1b[g][:, csl], tsc[:],
                                                                pbig[:, 512:1024], ALU.mult),
                        deps=[s1, br] + z_readers)
            ap_prev = [s2]
        x1_done.append(ap_prev[0])
        x1_done = x1_done[-NG:] if False else x1_done
    x1_fin = x1_done[-1]
    pr.close("ln0")

    dbg_last = None
    if "dx1" in dbg:
        prevd = None
        for g in range(NG):
            for tch in range(NTC):
                cc = pr.dve(lambda e, g=g, tch=tch: e.tensor_copy(
                    tsc[:], x1b[g][:, tch * 512:(tch + 1) * 512]), deps=[x1_fin, prevd])
                prevd = pr.dma(lambda e, g=g, tch=tch: e.dma_start(
                    dbg["dx1"][g * 128:(g + 1) * 128, tch * 512:(tch + 1) * 512], tsc[:]), deps=[cc])
        dbg_last = prevd
        pr.close("dbg_x1")

    NTT = L // 128
    SCR = SCRATCH
    qp = sb("qp", (128, L), BF16, at=SCR)
    kp = sb("kp", (128, L), BF16, at=SCR + L * 2)
    vt = sb("vt", (128, NTT * 130), BF16, at=SCR + L * 4)
    VT_END = SCR + L * 4 + ((NTT * 130 * 2 + 31) // 32) * 32
    prb = sb("prb", (128, 4096), BF16, at=VT_END + 24 * 1024)
    wqp2 = [sb(f"wqp{i}", (128, 1024), BF16, at=VT_END + 4096 + i * 13568) for i in range(2)]
    wkp2 = [sb(f"wkp{i}", (128, 1024), BF16, at=VT_END + 6144 + i * 13568) for i in range(2)]
    wvp2 = [sb(f"wvp{i}", (128, 1024), BF16, at=VT_END + 8192 + i * 13568) for i in range(2)]
    swpb2 = [sb(f"swpb{i}", (128, 2), BF16, at=VT_END + 10240 + i * 13568) for i in range(2)]
    rsch = sb("rsch", (2, 512), F32, at=VT_END + 10304)
    sqc = sb("sqc", (128, 512), BF16, at=VT_END + 12352)
    qnm = sb("qnm", (2, 16), F32, at=VT_END + 13376)
    rbs = sb("rbs", (64, 1024), F32, at=VT_END + 13440)
    negc = sb("negc", (128, 2), F32, at=VT_END + 17536)
    crow = sb("crow", (1, 2), F32, at=VT_END + 17600)

    ldw = {}
    out_done = [[None] * NTC for _ in range(NG)]
    x1_users = [x1_fin]
    oa_off = [Z_OFF + g * L * 4 + L * 2 for g in range(NG)]
    out_all = [sb(f"oa{g}", (128, L), BF16, at=oa_off[g]) for g in range(NG)]

    vt_ones = None
    prev_pair_ops = [x1_fin, dbg_last]
    wbuf_users = [[x1_fin, dbg_last], [x1_fin, dbg_last]]
    wload = {}
    for hp in range(NG):
        pr2 = pr
        wpar = hp % 2
        wqp, wkp, wvp, swpb = wqp2[wpar], wkp2[wpar], wvp2[wpar], swpb2[wpar]
        if hp in wload:
            lw1, lw2, lw3, lw4 = wload.pop(hp)
        else:
            lw1 = pr.dmag(lambda e, hp=hp, wqp=wqp: e.dma_start(
                wqp[:].rearrange("p (g c) -> p g c", c=128), dt["wq"][hp].rearrange("(g p) c -> p g c", p=128)), deps=wbuf_users[wpar])
            lw2 = pr.dmag(lambda e, hp=hp, wkp=wkp: e.dma_start(
                wkp[:].rearrange("p (g c) -> p g c", c=128), dt["wk"][hp].rearrange("(g p) c -> p g c", p=128)), deps=wbuf_users[wpar])
            lw3 = pr.dmag(lambda e, hp=hp, wvp=wvp: e.dma_start(
                wvp[:].rearrange("p (g c) -> p g c", c=128), dt["wv"][hp].rearrange("(g p) c -> p g c", p=128)), deps=wbuf_users[wpar])
            lw4 = pr.dmag(lambda e, hp=hp, swpb=swpb: e.dma_start(swpb[:], dt["swp"][hp]), deps=wbuf_users[wpar])
        if vt_ones is None:
            m1_ = pr.dve(lambda e: e.memset(vt.rearrange("p (j c) -> p j c", c=130)[:, :, 64], 1.0),
                         deps=prev_pair_ops)
            vt_ones = pr.dve(lambda e: e.memset(vt.rearrange("p (j c) -> p j c", c=130)[:, :, 129], 1.0),
                             deps=[m1_])
        qev = [None] * NTC
        kev = [None] * NTC
        nm0 = pr.dve(lambda e: e.memset(qnm[:, 0:2], 0.0), deps=prev_pair_ops)
        nmdep = [nm0]
        vtr_done = []
        lastq = lastk = lastv = lastrm = None
        for tc in range(NTC):
            csl = slice(tc * 512, (tc + 1) * 512)
            for which, wsb, dst, lww in (("q", wqp, qp, lw1), ("k", wkp, kp, lw2)):
                pv = pbig[:, 0:512] if which == "q" else pbig[:, 512:1024]
                mms = []
                for g in range(NG):
                    mms.append(pr.pe(lambda e, g=g, wsb=wsb, pv=pv, csl=csl: e.matmul(
                        pv, wsb[:, g * 128:(g + 1) * 128], x1b[g][:, csl],
                        start=(g == 0), stop=(g == NG - 1)),
                        deps=([lww, x1_fin] + prev_pair_ops) if g == 0 else []))
                ev = pr.dve(lambda e, dst=dst, pv=pv, csl=csl: e.tensor_copy(dst[:, csl], pv),
                            deps=[mms[-1]])
                if which == "q":
                    qev[tc] = ev
                    lastq = mms[-1]
                else:
                    kev[tc] = ev
                    lastk = mms[-1]
                x1_users.append(mms[0])
            vmm = []
            for g in range(NG):
                vmm.append(pr.pe(lambda e, g=g, csl=csl, wvp=wvp: e.matmul(
                    pbig[:, 1024:1536], wvp[:, g * 128:(g + 1) * 128], x1b[g][:, csl],
                    start=(g == 0), stop=(g == NG - 1)),
                    deps=([lw3, x1_fin] + prev_pair_ops) if g == 0 else []))
            x1_users.append(vmm[0])
            lastv = vmm[-1]
            vevac = pr.dve(lambda e, csl=csl: e.tensor_copy(sqc[:], pbig[:, 1024:1536]), deps=[vmm[-1]])
            for jj in range(4):
                j = tc * 4 + jj
                tp = pr.pe(lambda e, jj=jj, j=j: e.transpose(
                    pbt[:, (j % 8) * 128:(j % 8 + 1) * 128], sqc[:, jj * 128:(jj + 1) * 128], identb[:]),
                    deps=[vevac, ld["identb"]] + vtr_done[-8:-7])
                cpA = pr.dve(lambda e, j=j: e.tensor_copy(
                    vt[:, j * 130:j * 130 + 64], pbt[:, (j % 8) * 128:(j % 8) * 128 + 64]),
                    deps=[tp, vt_ones])
                cpB = pr.dve(lambda e, j=j: e.tensor_copy(
                    vt[:, j * 130 + 65:j * 130 + 129], pbt[:, (j % 8) * 128 + 64:(j % 8 + 1) * 128]),
                    deps=[tp])
                vtr_done.append(cpB)
            rmm = pr.pe(lambda e, csl=csl, swpb=swpb: e.matmul(pb7[0:2, :], swpb[:], qp[:, csl],
                                                              start=True, stop=True),
                        deps=[qev[tc], lw4])
            lastrm = rmm
            rsg = pr.act(lambda e: e.activation(rsch[:], pb7[0:2, :], AF.Sigmoid), deps=[rmm])
            rsc2 = pr.dve(lambda e: e.tensor_scalar(rsch[:], rsch[:], 0.25, None, ALU.mult), deps=[rsg])
            rbc = pr.pe(lambda e: e.matmul(pb7[:], bc2[:], rsch[:], start=True, stop=True),
                        deps=[rsc2, ld["bc2"], rsg])
            qsc = pr.dve(lambda e, csl=csl: e.tensor_tensor(qp[:, csl], qp[:, csl], pb7[:], ALU.mult),
                         deps=[rbc])
            qev[tc] = qsc
            sq1 = pr.act(lambda e, csl=csl: e.activation(sqc[:], qp[:, csl], AF.Square),
                         deps=[qsc, vevac, tp])
            nm1 = pr.pe(lambda e: e.matmul(pb7[0:2, :], swones_b[:], sqc[:], start=True, stop=True),
                        deps=[sq1, rbc, ld["swones_b"]])
            nr1 = pr.dve(lambda e: e.tensor_reduce(qnm[:, 2:3], pb7[0:2, :], AX, ALU.max),
                         deps=[nm1] + nmdep)
            nx1 = pr.dve(lambda e: e.tensor_tensor(qnm[:, 0:1], qnm[:, 0:1], qnm[:, 2:3], ALU.max),
                         deps=[nr1])
            sq2 = pr.act(lambda e, csl=csl: e.activation(sqc[:], kp[:, csl], AF.Square),
                         deps=[kev[tc], nm1])
            nm2 = pr.pe(lambda e: e.matmul(pb7[0:2, :], swones_b[:], sqc[:], start=True, stop=True),
                        deps=[sq2, nr1])
            nr2 = pr.dve(lambda e: e.tensor_reduce(qnm[:, 3:4], pb7[0:2, :], AX, ALU.max),
                         deps=[nm2, nx1])
            nx2 = pr.dve(lambda e: e.tensor_tensor(qnm[:, 1:2], qnm[:, 1:2], qnm[:, 3:4], ALU.max),
                         deps=[nr2])
            nmdep = [nx2]
        wbuf_users[wpar] = [lastq, lastk, lastv, lastrm]
        if hp + 1 < NG:
            npar = (hp + 1) % 2
            nq, nk, nv, nsw = wqp2[npar], wkp2[npar], wvp2[npar], swpb2[npar]
            wload[hp + 1] = (
                pr.dmag(lambda e, hp=hp, nq=nq: e.dma_start(
                    nq[:].rearrange("p (g c) -> p g c", c=128),
                    dt["wq"][hp + 1].rearrange("(g p) c -> p g c", p=128)),
                    deps=wbuf_users[npar]),
                pr.dmag(lambda e, hp=hp, nk=nk: e.dma_start(
                    nk[:].rearrange("p (g c) -> p g c", c=128),
                    dt["wk"][hp + 1].rearrange("(g p) c -> p g c", p=128)),
                    deps=wbuf_users[npar]),
                pr.dmag(lambda e, hp=hp, nv=nv: e.dma_start(
                    nv[:].rearrange("p (g c) -> p g c", c=128),
                    dt["wv"][hp + 1].rearrange("(g p) c -> p g c", p=128)),
                    deps=wbuf_users[npar]),
                pr.dmag(lambda e, hp=hp, nsw=nsw: e.dma_start(nsw[:], dt["swp"][hp + 1]),
                        deps=wbuf_users[npar]),
            )
        cb1 = pr.act(lambda e: e.activation(qnm[:, 4:6], qnm[:, 0:2], AF.Sqrt), deps=nmdep)
        cb2 = pr.dve(lambda e: e.tensor_tensor(qnm[:, 6:7], qnm[:, 4:5], qnm[:, 5:6], ALU.mult),
                     deps=[cb1])
        ctp = pr.pe(lambda e: e.transpose(pbig[0:1, 0:2], qnm[:, 6:7], identf[0:2, 0:2]),
                    deps=[cb2, ld["identf"]])
        ccp = pr.dve(lambda e: e.tensor_copy(crow[:], pbig[0:1, 0:2]), deps=[ctp])
        cmx = pr.dve(lambda e: e.tensor_reduce(crow[0:1, 0:1], crow[:], AX, ALU.max), deps=[ccp])
        cb3a = pr.pe(lambda e: e.matmul(pb7[:, 0:1], onesr_f[:], crow[0:1, 0:1], start=True, stop=True),
                     deps=[cmx, ld["onesr_f"]])
        cb4 = pr.dve(lambda e: e.tensor_scalar(negc[:, 0:1], pb7[:, 0:1], -1.0, None, ALU.mult),
                     deps=[cb3a])
        prev_pair_ops = [cb4]
        avA_last = None
        carry_exps = (None, None)
        carry_avs = (None,) * 4
        pending_evac = None
        def emit_evac(pend):
            nonlocal avA_last
            aA, aB, csl, tc_ = pend
            rc1 = pr.dve(lambda e: e.reciprocal(rbs[0:1, 0:512], pb4[64:65, :]), deps=[aA])
            rc2 = pr.dve(lambda e: e.reciprocal(rbs[0:1, 512:1024], pb5[64:65, :]),
                         deps=[aB, rc1])
            bA = pr.pe(lambda e: e.matmul(pb7[0:64, :], onesr_f[0:1, 0:64],
                                          rbs[0:1, 0:512], start=True, stop=True),
                       deps=[rc1])
            bB = pr.pe(lambda e: e.matmul(pb7[64:128, :], onesr_f[0:1, 0:64],
                                          rbs[0:1, 512:1024], start=True, stop=True), deps=[rc2])
            cA = pr.dve(lambda e: e.tensor_copy(rbs[:, 0:512], pb7[0:64, :]), deps=[bA, rc2])
            cB = pr.dve(lambda e: e.tensor_copy(rbs[:, 512:1024], pb7[64:128, :]),
                        deps=[bB, cA])
            oA = pr.dve(lambda e, hp=hp, csl=csl: e.tensor_tensor(
                out_all[hp][0:64, csl], rbs[:, 0:512], pb4[0:64, :], ALU.mult), deps=[cA])
            oB = pr.dve(lambda e, hp=hp, csl=csl: e.tensor_tensor(
                out_all[hp][64:128, csl], rbs[:, 512:1024], pb5[0:64, :], ALU.mult),
                deps=[cB, oA])
            out_done[hp][tc_] = oB
            avA_last = oB
            return oB
        for tc in range(NTC):
            csl = slice(tc * 512, (tc + 1) * 512)
            exp_prev = [avA_last if carry_exps[0] is None else carry_exps[0],
                        avA_last if carry_exps[1] is None else carry_exps[1]]
            war_prev = [avA_last if c is None else c for c in carry_avs]
            exps = [None] * NTT
            avs = [None] * NTT
            def em_scores(j, dep_extra=(), csl=csl, exp_prev=exp_prev, tc=tc):
                par = j % 2
                sA = pbig[:, par * 1024:par * 1024 + 512]
                sB = pbig[:, par * 1024 + 512:par * 1024 + 1024]
                jsl = slice(j * 128, (j + 1) * 128)
                mA = pr.pe(lambda e, jsl=jsl, sA=sA, csl=csl: e.matmul(
                    sA, kp[0:64, jsl], qp[0:64, csl], start=True, stop=True),
                    deps=[qev[tc], kev[tc], cb4, exp_prev[par]] + list(dep_extra))
                mB = pr.pe(lambda e, jsl=jsl, sB=sB, csl=csl: e.matmul(
                    sB, kp[64:128, jsl], qp[64:128, csl], start=True, stop=True), deps=())
                return mA, mB
            def em_exp(j, mB, exps=exps, avs=avs, exp_prev=exp_prev, war_prev=war_prev):
                par = j % 2
                par4 = j % 4
                war = avs[j - 4][1] if j >= 4 else war_prev[par4]
                ex = pr.act(lambda e, par=par, par4=par4: e.activation(
                    prb[:, par4 * 1024:(par4 + 1) * 1024], pbig[:, par * 1024:(par + 1) * 1024],
                    AF.Exp, bias=negc[:, 0:1]), deps=[mB, war])
                exps[j] = ex
                exp_prev[par] = ex
                return ex
            def em_av(j, exps=exps, avs=avs):
                par4 = j % 4
                aA = pr.pe(lambda e, j=j, par4=par4: e.matmul(
                    pb4[0:65, :], vt[:, j * 130:j * 130 + 65],
                    prb[:, par4 * 1024:par4 * 1024 + 512],
                    start=(j == 0), stop=(j == NTT - 1)),
                    deps=[exps[j]] + ([avA_last] if j == 0 else []) + vtr_done[-1:])
                aB = pr.pe(lambda e, j=j, par4=par4: e.matmul(
                    pb5[0:65, :], vt[:, j * 130 + 65:j * 130 + 130],
                    prb[:, par4 * 1024 + 512:par4 * 1024 + 1024],
                    start=(j == 0), stop=(j == NTT - 1)), deps=())
                avs[j] = (aA, aB)
                return aA, aB
            for j in range(4):
                mA, mB = em_scores(j)
                em_exp(j, mB)
            if pending_evac is not None:
                emit_evac(pending_evac)
                pending_evac = None
            em_av(0)
            em_av(1)
            em_av(2)
            for j in range(4, NTT):
                mA, mB = em_scores(j)
                em_exp(j, mB)
                aA, aB = em_av(j - 1)
            aA, aB = em_av(NTT - 1)
            carry_exps = (exps[NTT - 2], exps[NTT - 1])
            carry_avs = (avs[NTT - 4][1], avs[NTT - 3][1], avs[NTT - 2][1], avs[NTT - 1][1])
            pending_evac = (aA, aB, csl, tc)
        oB_l = emit_evac(pending_evac)
        pending_evac = None
        prev_pair_ops = [oB_l, carry_avs[-2], carry_avs[-1], carry_exps[0], carry_exps[1]]
        pr.close(f"attn_p{hp}")

    wg_sb = sb("wg_sb", (128, 8192), BF16, at=SCR)
    gbuf = sb("gbuf", (128, 4096), BF16, at=SCR + 16 * 1024)
    lwg = pr.dmag(lambda e: e.dma_start(wg_sb[:].rearrange("p (g c) -> p g c", c=1024), dt["wgate"].rearrange("(g p) c -> p g c", p=128)),
                  deps=[qev[0]])
    og_done = [[None] * NTC for _ in range(NG)]
    prev_sig = [None] * NG
    for tc in range(NTC):
        csl = slice(tc * 512, (tc + 1) * 512)
        sig = [None] * NG
        for o_ in range(NG):
            pv = pbig[:, (o_ % 4) * 512:(o_ % 4) * 512 + 512]
            mms = []
            for g in range(NG):
                mms.append(pr.pe(lambda e, g=g, o_=o_, pv=pv, csl=csl: e.matmul(
                    pv, wg_sb[:, g * 1024 + o_ * 128:g * 1024 + (o_ + 1) * 128], out_all[g][:, csl],
                    start=(g == 0), stop=(g == NG - 1)),
                    deps=[lwg, out_done[g][tc]] + ([sig[o_ - 4]] if o_ >= 4
                                                    else [prev_sig[o_ + 4]])))
            sig[o_] = pr.act(lambda e, o_=o_, pv=pv: e.activation(
                gbuf[:, o_ * 512:(o_ + 1) * 512], pv, AF.Sigmoid, bias=ppc(P_GATEB)[:, o_:o_ + 1]),
                deps=[mms[-1], og_done[o_][tc - 1] if tc else None])
        for o_ in range(NG):
            og_done[o_][tc] = pr.dve(lambda e, o_=o_, csl=csl: e.tensor_tensor(
                out_all[o_][:, csl], out_all[o_][:, csl], gbuf[:, o_ * 512:(o_ + 1) * 512], ALU.mult),
                deps=[sig[NG - 1], og_done[o_][tc - 1] if tc else None])
        prev_sig = sig
    pr.close("gate")

    wo_sb = sb("wo_sb", (128, 8192), BF16, at=SCR)
    nbch = sb("nbch", (128, 2048), F32, at=SCR + 40 * 1024)
    z2sb2 = [sb(f"z2sb{i}", (128, 1024), F32, at=SCR + (16 if i == 0 else 32) * 1024)
             for i in range(2)]
    sqn = sb("sqn", (128, 1024), F32, at=SCR + 20 * 1024)
    x2ff = sb("x2ff", (128, 1024), F32, at=SCR + 24 * 1024)
    x1ts = sb("x1ts", (128, 1024), BF16, at=SCR + 36 * 1024)
    lgs = sb("lgs", (128, 8), F32, at=SCR + 30 * 1024)
    lgs2 = sb("lgs2", (128, 8), F32, at=SCR + 30 * 1024 + 512)
    lgs2b = sb("lgs2b", (128, 8), BF16, at=SCR + 30 * 1024 + 768)
    tkw = sb("tkw", (128, 16), F32, at=SCR + 30 * 1024 + 64)
    mgwf = sb("mgwf", (128, 64), F32, at=SCR + 31 * 1024)
    mwr = sb("mwr", (8, L), BF16, at=SCR + 48 * 1024)
    w1p_all = sb("w1p_all", (128, 32), F32, at=SCR + 56 * 1024)
    b2b = sb("b2b", (8, 1024), BF16, at=SCR + 56 * 1024 + 256)
    msk_all = sb("msk_all", (128, 8 * NTT), F32, at=SCR + 60 * 1024)
    x2fm = [sb(f"x2fm{g}", (128, L), BF16, at=Z_OFF + g * L * 4) for g in range(NG)]

    lwo = pr.dmag(lambda e: e.dma_start(wo_sb[:].rearrange("p (g c) -> p g c", c=1024), dt["wout"].rearrange("(g p) c -> p g c", p=128)),
                  deps=[lwg])
    lnb = pr.dma(lambda e: e.dma_start(nbch[:], dt["nrep"][:, 0:2 * D]))
    lmg = pr.dma(lambda e: e.dma_start(mgwf[:].rearrange("p (g c) -> p g c", c=8), dt["mgw"].rearrange("(g p) c -> p g c", p=128)))
    lb2 = pr.dmag(lambda e: e.dma_start(b2b[:], dt["b2"][:]))
    rows_d = pr.dve(lambda e: e.memset(tkw[:, 0:16], 0.0))
    prev_tt = [rows_d]
    x1s_dma = None
    c4_dbg_dmas = []
    x2_done = [[None] * NTT for _ in range(NG)]
    msk_done = [None] * NTT
    st = {"z2b": [None] * NTT, "cpt": [None] * NTT, "ap3": [None] * NTT,
          "ev2": [None] * NTT, "lg1": [None] * NTT, "mcp": [None] * NTT,
          "tps2": [None] * NTT, "mmz": [None] * NTT, "sqr": []}

    def stage1pe(tt):
        tsl = slice(tt * 128, (tt + 1) * 128)
        mm_z = []
        for dc in range(2):
            for g in range(NG):
                dps = []
                if g == 0:
                    dps = [lwo, og_done[g][-1], st["z2b"][tt - 1] if tt else rows_d]
                mm_z.append(pr.pe(lambda e, g=g, dc=dc, tsl=tsl: e.matmul(
                    pbig[:, dc * 512:(dc + 1) * 512],
                    out_all[g][:, tsl], wo_sb[:, g * 1024 + dc * 512:g * 1024 + (dc + 1) * 512],
                    start=(g == 0), stop=(g == NG - 1)), deps=dps))
        tps = []
        for g in range(NG):
            tps.append(pr.pe(lambda e, g=g, tsl=tsl: e.transpose(
                pbt[:, g * 128:(g + 1) * 128], x1b[g][:, tsl], identb[:]),
                deps=[x1_fin, st["cpt"][tt - 1] if tt else rows_d]))
        x1_users.append(tps[0])
        st["tps"] = tps
        st["mmz"][tt] = mm_z

    def stage1dve(tt):
        nonlocal x1s_dma
        tsl = slice(tt * 128, (tt + 1) * 128)
        z2v = z2sb2[tt % 2]
        mm_z = st["mmz"][tt]
        cpt = pr.dve(lambda e: e.tensor_copy(x1ts[:], pbt[:]), deps=[st["tps"][-1], x1s_dma])
        x1s_dma = pr.dma(lambda e, tsl=tsl: e.dma_start(x1tok[tsl, :], x1ts[:]), deps=[cpt])
        war = st["tps2"][tt - 2][-1] if tt >= 2 else None
        z2a = pr.dve(lambda e, z2v=z2v: e.tensor_tensor(
            z2v[:, 0:512], x1ts[:, 0:512], pbig[:, 0:512], ALU.add),
            deps=[cpt, mm_z[7], war])
        z2b = pr.dve(lambda e, z2v=z2v: e.tensor_tensor(
            z2v[:, 512:1024], x1ts[:, 512:1024], pbig[:, 512:1024], ALU.add),
            deps=[cpt, mm_z[15], z2a])
        st["cpt"][tt] = cpt
        st["z2b"][tt] = z2b

    def stage2(tt):
        tsl = slice(tt * 128, (tt + 1) * 128)
        z2v = z2sb2[tt % 2]
        z2b = st["z2b"][tt]
        rd = pr.dve(lambda e, z2v=z2v: e.tensor_reduce(tkw[:, 0:1], z2v[:], AX, ALU.add),
                    deps=[z2b, st.get("ddtk")])
        sq = pr.act(lambda e, z2v=z2v: e.activation(sqn[:], z2v[:], AF.Square,
                                                    accum_out=tkw[:, 1:2]),
                    deps=[z2b, st["ap3"][tt - 1] if tt else None, st.get("ddtk")] + st["sqr"])
        st["sqr"] = []
        v1 = pr.dve(lambda e: e.tensor_scalar(tkw[:, 2:3], tkw[:, 0:1], 1.0 / D, None, ALU.mult),
                    deps=[rd])
        v2 = pr.dve(lambda e: e.tensor_scalar(tkw[:, 3:4], tkw[:, 1:2], 1.0 / D, None, ALU.mult),
                    deps=[sq, v1])
        v3 = pr.dve(lambda e: e.tensor_tensor(tkw[:, 4:5], tkw[:, 2:3], tkw[:, 2:3], ALU.mult),
                    deps=[v2])
        v4 = pr.dve(lambda e: e.tensor_tensor(tkw[:, 3:4], tkw[:, 3:4], tkw[:, 4:5], ALU.subtract),
                    deps=[v3])
        v5 = pr.act(lambda e: e.activation(tkw[:, 3:4], tkw[:, 3:4], AF.Ln, bias=cvals[:, 0:1]),
                    deps=[v4])
        v6 = pr.act(lambda e: e.activation(tkw[:, 3:4], tkw[:, 3:4], AF.Exp, scale=-0.5), deps=[v5])
        ap1 = pr.dve(lambda e, z2v=z2v: e.tensor_scalar(sqn[:], z2v[:], tkw[:, 2:3], tkw[:, 3:4],
                                                        ALU.subtract, ALU.mult), deps=[v6, sq])
        ap2 = pr.dve(lambda e: e.tensor_tensor(sqn[:], sqn[:], nbch[:, 0:D], ALU.mult),
                     deps=[ap1, lnb])
        ap3 = pr.dve(lambda e, z2v=z2v: e.tensor_tensor(z2v[:], sqn[:], nbch[:, D:2 * D], ALU.add),
                     deps=[ap2])
        st["ap3"][tt] = ap3
        tps2 = []
        for g in range(NG):
            tps2.append(pr.pe(lambda e, g=g, z2v=z2v: e.transpose(
                pbig[:, 1024 + g * 128:1024 + (g + 1) * 128], z2v[:, g * 128:(g + 1) * 128],
                identf[:]), deps=[ap3, ld["identf"]] +
                ([tps2[-1]] if g else [st["ev2"][tt - 1] if tt else None])))
        st["tps2"][tt] = tps2
        ev2 = pr.dve(lambda e: e.tensor_copy(x2ff[:], pbig[:, 1024:2048]), deps=[tps2[-1]])
        st["ev2"][tt] = ev2
        cp2 = []
        for g in range(NG):
            cp2.append(pr.dve(lambda e, g=g, tsl=tsl: e.tensor_copy(
                x2fm[g][:, tsl], x2ff[:, g * 128:(g + 1) * 128]),
                deps=[ev2]))
            x2_done[g][tt] = cp2[-1]
        extra_lmm_dep = []
        if "dxf" in dbg and tt == 1:
            ddx = pr.dma(lambda e: e.dma_start(dbg["dxf"][:], x2ff[:]), deps=[ev2])
            extra_lmm_dep = [ddx]
            c4_dbg_dmas.append(ddx)
        lmm = []
        for g in range(NG):
            lmm.append(pr.pe(lambda e, g=g: e.matmul(
                pb7[:, 256:264], x2ff[:, g * 128:(g + 1) * 128], mgwf[:, g * 8:(g + 1) * 8],
                start=(g == 0), stop=(g == NG - 1)),
                deps=[ev2, lmg, tps2[-1], st["lg1"][tt - 1] if tt else None] + extra_lmm_dep))
        lg1 = pr.dve(lambda e: e.tensor_tensor(lgs[:], mgb[:], pb7[:, 256:264], ALU.add),
                     deps=[lmm[-1], ld["mgb"], st.get("ddlg")])
        st["lg1"][tt] = lg1
        m1 = pr.dve(lambda e: e.tensor_reduce(tkw[:, 5:6], lgs[:], AX, ALU.max), deps=[lg1])
        m1n = pr.dve(lambda e: e.tensor_scalar(tkw[:, 6:7], tkw[:, 5:6], -1.0, None, ALU.mult),
                     deps=[m1])
        eq = pr.dve(lambda e: e.tensor_scalar(lgs2[:], lgs[:], tkw[:, 5:6],
                                              None, ALU.is_equal), deps=[m1])
        mkd = pr.dve(lambda e: e.scalar_tensor_tensor(lgs2[:], lgs2[:], -1e30, lgs[:],
                                                      ALU.mult, ALU.add), deps=[eq])
        m2 = pr.dve(lambda e: e.tensor_reduce(tkw[:, 7:8], lgs2[:], AX, ALU.max), deps=[mkd])
        ge2 = pr.dve(lambda e: e.tensor_scalar(lgs2[:], lgs[:], tkw[:, 7:8], None, ALU.is_ge),
                     deps=[m2])
        cand = pr.dve(lambda e: e.tensor_tensor(lgs2[:], lgs2[:], iota8[:], ALU.mult),
                      deps=[ge2, ld["iota8"]])
        cho = pr.dve(lambda e: e.tensor_reduce(tkw[:, 8:9], lgs2[:], AX, ALU.max), deps=[cand])
        e2 = pr.act(lambda e: e.activation(tkw[:, 9:10], tkw[:, 7:8], AF.Exp, bias=tkw[:, 6:7]),
                    deps=[m2, m1n])
        s_ = pr.dve(lambda e: e.tensor_scalar(tkw[:, 10:11], tkw[:, 9:10], 1.0, None, ALU.add),
                    deps=[e2])
        rcp = pr.dve(lambda e: e.reciprocal(tkw[:, 11:12], tkw[:, 10:11]), deps=[s_])
        p2_ = pr.dve(lambda e: e.tensor_tensor(tkw[:, 12:13], tkw[:, 9:10], tkw[:, 11:12], ALU.mult),
                     deps=[rcp])
        wsm = pr.dve(lambda e: e.tensor_tensor(tkw[:, 13:14], tkw[:, 11:12], tkw[:, 12:13], ALU.add),
                     deps=[p2_])
        wp1 = pr.dve(lambda e, tt=tt: e.tensor_scalar(w1p_all[:, tt:tt + 1], tkw[:, 13:14], 1.0, None,
                                                      ALU.add), deps=[wsm])
        st["wp1"] = wp1
        msk = pr.dve(lambda e: e.tensor_scalar(lgs2[:], iota8[:], tkw[:, 8:9], None, ALU.is_equal),
                     deps=[cho, cand])
        mw1 = pr.dve(lambda e: e.tensor_scalar(lgs2[:], lgs2[:], tkw[:, 13:14], None, ALU.mult),
                     deps=[msk, wsm])
        mw2 = pr.dve(lambda e: e.tensor_tensor(lgs2b[:], lgs2[:], rscr[:], ALU.mult),
                     deps=[mw1, ld["rsc"]])
        msA = pr.dve(lambda e, tt=tt: e.tensor_tensor(msk_all[:, tt * 8:(tt + 1) * 8],
                                                      lgs2[:], rscr[:], ALU.mult),
                     deps=[mw1, ld["rsc"]])
        mtp = pr.pe(lambda e: e.matmul(pb7[0:8, 0:128], lgs2b[:], identb[:],
                                       start=True, stop=True),
                    deps=[mw2, ld["identb"], st["mcp"][tt - 1] if tt else None])
        mcp = pr.dve(lambda e, tsl=tsl: e.tensor_copy(mwr[:, tsl], pb7[0:8, 0:128]), deps=[mtp])
        st["mcp"][tt] = mcp
        msk_done[tt] = mcp
        if "dxf" in dbg and tt == 1:
            ddp1 = pr.dve(lambda e: e.tensor_copy(sqn[:, 0:8], pb7[:, 256:264]),
                          deps=[lmm[-1], ap1, sq])
            ddp2 = pr.dma(lambda e: e.dma_start(dbg["dps"][:], sqn[:, 0:8]), deps=[ddp1])
            c4_dbg_dmas.append(ddp2)
            st["sqr"] += [ddp1, ddp2]
        if "dtk" in dbg:
            dd2 = pr.dma(lambda e, tt=tt: e.dma_start(dbg["dtk"][:, tt * 16:(tt + 1) * 16],
                                                      tkw[:]), deps=[wp1, mw2, mcp])
            dd4 = pr.dma(lambda e, tt=tt: e.dma_start(dbg["dlg"][:, tt * 8:(tt + 1) * 8],
                                                      lgs[:]), deps=[lg1, m1, mkd])
            c4_dbg_dmas.extend([dd2, dd4])
            st["ddtk"] = dd2
            st["ddlg"] = dd4

    stage1pe(0)
    stage1dve(0)
    for tt in range(1, NTT):
        stage1pe(tt)
        stage2(tt - 1)
        stage1dve(tt)
    stage2(NTT - 1)
    pr.close("c4")

    if "dx2" in dbg or "dmwr" in dbg or "doa" in dbg:
        pdmas = []
        for g in range(NG):
            if "dx2" in dbg:
                pdmas.append(pr.dmag(lambda e, g=g: e.dma_start(
                    dbg["dx2"][g * 128:(g + 1) * 128, :], x2fm[g][:]), deps=[x2_done[g][-1]]))
            if "doa" in dbg:
                pdmas.append(pr.dmag(lambda e, g=g: e.dma_start(
                    dbg["doa"][g * 128:(g + 1) * 128, :], out_all[g][:]), deps=[og_done[g][-1]]))
        if "dmwr" in dbg:
            pdmas.append(pr.dmag(lambda e: e.dma_start(dbg["dmwr"][:], mwr[:]), deps=[msk_done[-1]]))
        pr.close("dbg_c")
        moe_wait = pdmas + c4_dbg_dmas[-20:] + [x1s_dma]
    else:
        moe_wait = c4_dbg_dmas[-20:] + [x1s_dma]

    TCH = 512
    NTC2 = L // TCH
    FD = 2048
    if L == 4096:
        w1h = [sb(f"w1h{g}", (128, FD), BF16, at=Z_OFF + g * L * 4 + L * 2) for g in range(NG)]
        w2h = [sb(f"w2h{f}", (128, D), BF16,
                  at=Z_OFF + (f // 2) * L * 4 + L * 2 + 4096 + (f % 2) * 2048) for f in range(16)]
    else:
        w1h = [sb(f"w1h{g}", (128, FD), BF16, at=SCR + 72 * 1024 + g * 4096) for g in range(NG)]
        w2h = [sb(f"w2h{f}", (128, D), BF16, at=SCR + 104 * 1024 + f * 2048) for f in range(16)]
    hhB = sb("hhB", (128, 16 * TCH), BF16, at=SCR)
    hgb = sb("hgb", (128, 16 * TCH), BF16, at=SCR + 16 * 1024)
    ysb = sb("ysb", (128, 1024), F32, at=SCR + 32 * 1024)
    mrsb = sb("mrsb", (128, 2 * TCH), F32, at=SCR + 36 * 1024)
    wrk2 = sb("wrk2", (128, 2 * TCH), F32, at=SCR + 40 * 1024)
    sq2m = sb("sq2m", (128, 2 * TCH), BF16, at=SCR + 44 * 1024)
    bcols = sb("bcols", (128, 48), F32, at=SCR + 46 * 1024)
    rm = sb("rm", (1, 3 * TCH), F32, at=10 * 1024)
    mrsb_user = None

    w1_users = [None] * NG
    w2_users = [None] * 16
    b_users = None
    yacc_done = [[None, None] for _ in range(NTT)]
    ysb_user = [None, None]
    prev_stats = [None, None]
    wrk_gl = [None, None]
    sq_mm = [None, None]
    prev_a1 = [None] * 16
    prev_py = None
    prev_w2tail = None
    pending = None
    step = 0

    def emit_w2(pend):
        nonlocal prev_py, prev_w2tail, ysb_user
        e_, tch, ap_last, lw2e_ = pend
        for tt2 in range(TCH // 128):
            ttg = tch * (TCH // 128) + tt2
            mm2s = []
            for dc in range(2):
                for ft in range(16):
                    dps = []
                    if tch == 0 and tt2 == 0 and dc == 0:
                        dps.append(lw2e_[ft])
                    if ft == 0:
                        dps += [ap_last, prev_py]
                    mm2s.append(pr.pe(lambda e, ft=ft, dc=dc, tt2=tt2: e.matmul(
                        pbig[:, 1024 + dc * 512:1024 + (dc + 1) * 512],
                        hgb[:, ft * TCH + tt2 * 128:ft * TCH + (tt2 + 1) * 128],
                        w2h[ft][:, dc * 512:(dc + 1) * 512],
                        start=(ft == 0), stop=(ft == 15)), deps=dps))
                    w2_users[ft] = mm2s[-1]
            dys = []
            for h in range(2):
                ev2 = pr.dve(lambda e, e_=e_, ttg=ttg, h=h: e.tensor_scalar(
                    ysb[:, h * 512:(h + 1) * 512], pbig[:, 1024 + h * 512:1536 + h * 512],
                    msk_all[:, ttg * 8 + e_:ttg * 8 + e_ + 1], None, ALU.mult),
                    deps=[mm2s[15 if h == 0 else 31], ysb_user[h]])
                dy = pr.dmag(lambda e, e_=e_, ttg=ttg, h=h: e.dma_start(
                    yacc[ttg * 128:(ttg + 1) * 128, h * 512:(h + 1) * 512],
                    ysb[:, h * 512:(h + 1) * 512],
                    accum_op=(ALU.bypass if e_ == 0 else ALU.add)),
                    deps=[ev2, yacc_done[ttg][h]])
                dys.append(dy)
                ysb_user[h] = dy
                prev_py = ev2
            yacc_done[ttg] = dys
            prev_w2tail = mm2s[-1]

    lw2e = None
    deferred = None

    def apply_step(ctx, ft):
        nonlocal mrsb_user, b_users
        wv = wrk2[:, (ft % 2) * TCH:(ft % 2) * TCH + TCH]
        a1 = pr.dve(lambda e, wv=wv, ft=ft: e.tensor_tensor(
            wv, hhB[:, ft * TCH:(ft + 1) * TCH], mrsb[:, 0:TCH], ALU.subtract),
            deps=[ctx["mbev"], wrk_gl[ft % 2], ctx["evB"][ft]])
        prev_a1[ft] = a1
        a2 = pr.dve(lambda e, wv=wv: e.tensor_tensor(wv, wv, mrsb[:, TCH:2 * TCH],
                                                     ALU.mult), deps=[ctx["rbev"], a1])
        mrsb_user = a2
        a3 = pr.dve(lambda e, wv=wv, ft=ft: e.tensor_scalar(
            wv, wv, bcols[:, 16 + ft:17 + ft], bcols[:, 32 + ft:33 + ft],
            ALU.mult, ALU.add), deps=[a2, ctx["lgc"], ctx["lbc2"]])
        gl = pr.act(lambda e, wv=wv, ft=ft: e.activation(
            hgb[:, ft * TCH:(ft + 1) * TCH], wv, AF.Gelu), deps=[a3, prev_w2tail])
        wrk_gl[ft % 2] = gl
        if ft == 15:
            ctx["pend"][2] = gl
            b_users = a3

    for e_ in range(NE):
        for tch in range(NTC2):
            if tch == 0:
                if deferred is not None:
                    for ft in range(16):
                        apply_step(deferred, ft)
                    deferred = None
                lw1e = [pr.dmag(lambda e, e_=e_, g=g: e.dma_start(
                    w1h[g][:], dt["w1"][e_, g * 128:(g + 1) * 128, :]), deps=[w1_users[g]])
                    for g in range(NG)]
                if pending is not None:
                    emit_w2(pending)
                    pending = None
                lw2e = [pr.dmag(lambda e, e_=e_, f=f: e.dma_start(
                    w2h[f][:], dt["w2"][e_, f * 128:(f + 1) * 128, :]), deps=[w2_users[f]])
                    for f in range(16)]
                lbc = pr.dma(lambda e, e_=e_: e.dma_start(bcols[:, 0:16], dt["b1p"][e_]),
                             deps=[b_users])
                lgc = pr.dma(lambda e, e_=e_: e.dma_start(bcols[:, 16:32], dt["lgp"][e_]),
                             deps=[b_users])
                lbc2 = pr.dma(lambda e, e_=e_: e.dma_start(bcols[:, 32:48], dt["lbp"][e_]),
                              deps=[b_users])
            csl = slice(tch * TCH, (tch + 1) * TCH)
            evB = [None] * 16
            sqe_l = [None] * 16
            nmm1 = nmm2 = None
            ctx = deferred
            deferred = None
            for ft in range(16):
                pv = pbig[:, (ft % 2) * TCH:(ft % 2) * TCH + TCH]
                sqs = sq2m[:, (ft % 2) * TCH:(ft % 2) * TCH + TCH]
                mms = []
                for g in range(NG):
                    dps = []
                    if ft == 0 and tch == 0:
                        dps.append(lw1e[g])
                    if g == 0:
                        dps += [x2_done[g][-1]] + ([evB[ft - 2], sqe_l[ft - 2]] if ft >= 2 else [])
                    mms.append(pr.pe(lambda e, g=g, ft=ft, pv=pv, csl=csl: e.matmul(
                        pv, w1h[g][:, ft * 128:(ft + 1) * 128], x2fm[g][:, csl],
                        start=(g == 0), stop=(g == NG - 1)), deps=dps))
                    w1_users[g] = mms[-1]
                if ctx is not None:
                    apply_step(ctx, ft)
                evB[ft] = pr.act(lambda e, ft=ft, pv=pv: e.activation(
                    hhB[:, ft * TCH:(ft + 1) * TCH], pv, AF.Identity, bias=bcols[:, ft:ft + 1]),
                    deps=[mms[-1], lbc, prev_a1[ft]])
                sqe_l[ft] = pr.act(lambda e, ft=ft, pv=pv, sqs=sqs: e.activation(
                    sqs, pv, AF.Square, bias=bcols[:, ft:ft + 1]),
                    deps=[mms[-1], lbc, sq_mm[ft % 2]])
                nmm1 = pr.pe(lambda e, ft=ft: e.matmul(
                    pb4[0:1, 0:TCH], onesc_b[:], hhB[:, ft * TCH:(ft + 1) * TCH],
                    start=(ft == 0), stop=(ft == 15)),
                    deps=[evB[ft]] + ([prev_stats[0]] if ft == 0 else []))
                nmm2 = pr.pe(lambda e, ft=ft, sqs=sqs: e.matmul(
                    pb5[0:1, 0:TCH], onesc_b[:], sqs,
                    start=(ft == 0), stop=(ft == 15)),
                    deps=[sqe_l[ft]] + ([prev_stats[1]] if ft == 0 else []))
                sq_mm[ft % 2] = nmm2
            s0 = pr.dve(lambda e: e.tensor_scalar(rm[0:1, 0:TCH], pb4[0:1, 0:TCH], 1.0 / FD, None,
                                                  ALU.mult), deps=[nmm1])
            s1 = pr.dve(lambda e: e.tensor_scalar(rm[0:1, TCH:2 * TCH], pb5[0:1, 0:TCH], 1.0 / FD,
                                                  None, ALU.mult), deps=[nmm2, s0])
            s2 = pr.dve(lambda e: e.tensor_tensor(rm[0:1, 2 * TCH:3 * TCH], rm[0:1, 0:TCH],
                                                  rm[0:1, 0:TCH], ALU.mult), deps=[s1])
            s3 = pr.dve(lambda e: e.tensor_tensor(rm[0:1, TCH:2 * TCH], rm[0:1, TCH:2 * TCH],
                                                  rm[0:1, 2 * TCH:3 * TCH], ALU.subtract), deps=[s2])
            s4 = pr.act(lambda e: e.activation(rm[0:1, TCH:2 * TCH], rm[0:1, TCH:2 * TCH], AF.Ln,
                                               bias=cvals[0:1, 0:1]), deps=[s3])
            s5 = pr.act(lambda e: e.activation(rm[0:1, TCH:2 * TCH], rm[0:1, TCH:2 * TCH], AF.Exp,
                                               scale=-0.5), deps=[s4])
            if pending is not None:
                emit_w2(pending)
                pending = None
            bm = None
            for hf in range(TCH // 256):
                bm = pr.pe(lambda e, hf=hf: e.matmul(
                    pb4[:, hf * 256:(hf + 1) * 256], onesr_f[:],
                    rm[0:1, hf * 256:hf * 256 + 256], start=True, stop=True),
                    deps=[s2] if hf == 0 else [])
            br = None
            for hf in range(TCH // 256):
                br = pr.pe(lambda e, hf=hf: e.matmul(
                    pb5[:, hf * 256:(hf + 1) * 256], onesr_f[:],
                    rm[0:1, TCH + hf * 256:TCH + hf * 256 + 256], start=True, stop=True),
                    deps=[s5, bm] if hf == 0 else [])
            mbev = pr.dve(lambda e: e.tensor_copy(mrsb[:, 0:TCH], pb4[:, 0:TCH]),
                          deps=[bm, mrsb_user] + (moe_wait if step == 0 else []))
            rbev = pr.dve(lambda e: e.tensor_copy(mrsb[:, TCH:2 * TCH], pb5[:, 0:TCH]),
                          deps=[br, mbev])
            prev_stats = [mbev, rbev]
            mrsb_user = None
            pend = [e_, tch, None, lw2e]
            deferred = {"mbev": mbev, "rbev": rbev, "lgc": lgc, "lbc2": lbc2,
                        "evB": evB, "pend": pend}
            pending = pend
            step += 1
    for ft in range(16):
        apply_step(deferred, ft)
    deferred = None
    emit_w2(pending)
    pending = None
    pr.close("moe")

    yacc_sb2 = [sb(f"yacc_sb{i}", (128, 1024), F32, at=SCR + (0 if i == 0 else 22 * 1024))
                for i in range(2)]
    z32 = [sb(f"z3{i}", (128, 1024), F32, at=SCR + (4 if i == 0 else 28) * 1024)
           for i in range(2)]
    sqe_ = sb("sqe", (128, 1024), F32, at=SCR + 8 * 1024)
    x2ts = sb("x2ts", (128, 1024), BF16, at=SCR + 12 * 1024)
    tke = sb("tke", (128, 8), F32, at=SCR + 14 * 1024)
    x1e2 = [sb(f"x1e{i}", (128, 1024), BF16, at=SCR + (16 if i == 0 else 26) * 1024)
            for i in range(2)]
    obuf = sb("obuf", (128, 1024), F32, at=SCR + 18 * 1024)
    lnb2 = pr.dma(lambda e: e.dma_start(nbch[:], dt["nrep"][:, 2 * D:4 * D]), deps=[])
    fin = []
    se = {"z3a": [None] * NTT, "z3c": [None] * NTT, "xcp": [None] * NTT,
          "ly": [None] * NTT, "lx1": [None] * NTT, "f1": [None] * NTT,
          "f3": [None] * NTT, "e2": [None] * NTT, "fa": [None] * NTT, "od": [None] * NTT}

    def stage1epe(tt):
        tsl = slice(tt * 128, (tt + 1) * 128)
        par = tt % 2
        ysb_e = yacc_sb2[par]
        ly = pr.dma(lambda e, tsl=tsl, ysb_e=ysb_e: e.dma_start(ysb_e[:], yacc[tsl, :]),
                    deps=yacc_done[tt] + [se["z3a"][tt - 2] if tt >= 2 else None])
        lx1 = pr.dma(lambda e, tsl=tsl, par=par: e.dma_start(x1e2[par][:], x1tok[tsl, :]),
                     deps=[x1s_dma, se["f1"][tt - 2] if tt >= 2 else None])
        b2m = [pr.pe(lambda e, dc=dc, tsl=tsl: e.matmul(
            pbig[:, dc * 512:(dc + 1) * 512], mwr[:, tsl], b2b[:, dc * 512:(dc + 1) * 512],
            start=True, stop=True),
            deps=[msk_done[tt], lb2, se["z3c"][tt - 1] if tt else None]) for dc in range(2)]
        tps = []
        for g in range(NG):
            tps.append(pr.pe(lambda e, g=g, tsl=tsl: e.transpose(
                pbt[:, g * 128:(g + 1) * 128], x2fm[g][:, tsl], identb[:]),
                deps=[x2_done[g][-1], se["xcp"][tt - 1] if tt else None]))
        se["ly"][tt] = ly
        se["lx1"][tt] = lx1
        se["tps"] = tps
        se["b2m"] = b2m

    def stage1edve(tt):
        par = tt % 2
        ysb_e = yacc_sb2[par]
        z3v = z32[par]
        b2m = se["b2m"]
        z3a = pr.dve(lambda e, tt=tt, z3v=z3v, ysb_e=ysb_e: e.scalar_tensor_tensor(
            z3v[:], pbt[:], w1p_all[:, tt:tt + 1], ysb_e[:], ALU.mult, ALU.add),
            deps=[se["tps"][-1], se["ly"][tt], st["wp1"],
                  se["f3"][tt - 2] if tt >= 2 else None])
        z3b = pr.dve(lambda e, z3v=z3v: e.tensor_tensor(
            z3v[:, 0:512], z3v[:, 0:512], pbig[:, 0:512], ALU.add), deps=[z3a, b2m[0]])
        z3c = pr.dve(lambda e, z3v=z3v: e.tensor_tensor(
            z3v[:, 512:1024], z3v[:, 512:1024], pbig[:, 512:1024], ALU.add),
            deps=[z3b, b2m[1]])
        se["xcp"][tt] = z3a
        se["z3a"][tt] = z3a
        se["z3c"][tt] = z3c

    def stage2e(tt):
        tsl = slice(tt * 128, (tt + 1) * 128)
        par = tt % 2
        z3v = z32[par]
        z3c = se["z3c"][tt]
        rd = pr.dve(lambda e, z3v=z3v: e.tensor_reduce(tke[:, 0:1], z3v[:], AX, ALU.add),
                    deps=[z3c])
        sq = pr.act(lambda e, z3v=z3v: e.activation(sqe_[:], z3v[:], AF.Square,
                                                    accum_out=tke[:, 1:2]),
                    deps=[z3c, se["f1"][tt - 1] if tt else None])
        v1 = pr.dve(lambda e: e.tensor_scalar(tke[:, 2:3], tke[:, 0:1], 1.0 / D, None, ALU.mult),
                    deps=[rd])
        v2 = pr.dve(lambda e: e.tensor_scalar(tke[:, 3:4], tke[:, 1:2], 1.0 / D, None, ALU.mult),
                    deps=[sq, v1])
        v3 = pr.dve(lambda e: e.tensor_tensor(tke[:, 4:5], tke[:, 2:3], tke[:, 2:3], ALU.mult),
                    deps=[v2])
        v4 = pr.dve(lambda e: e.tensor_tensor(tke[:, 3:4], tke[:, 3:4], tke[:, 4:5], ALU.subtract),
                    deps=[v3])
        v5 = pr.act(lambda e: e.activation(tke[:, 3:4], tke[:, 3:4], AF.Ln, bias=cvals[:, 0:1]),
                    deps=[v4])
        v6 = pr.act(lambda e: e.activation(tke[:, 3:4], tke[:, 3:4], AF.Exp, scale=-0.5), deps=[v5])
        e1 = pr.dve(lambda e, z3v=z3v: e.tensor_scalar(sqe_[:], z3v[:], tke[:, 2:3], tke[:, 3:4],
                                                       ALU.subtract, ALU.mult), deps=[v6, sq])
        e2_ = pr.dve(lambda e: e.tensor_tensor(sqe_[:], sqe_[:], nbch[:, 0:D], ALU.mult),
                     deps=[e1, lnb2])
        e3 = pr.dve(lambda e, z3v=z3v: e.tensor_tensor(z3v[:], sqe_[:], nbch[:, D:2 * D], ALU.add),
                    deps=[e2_])
        se["e2"][tt] = e2_
        f1 = pr.dve(lambda e, z3v=z3v, par=par: e.tensor_tensor(z3v[:], z3v[:], x1e2[par][:],
                                                                ALU.add),
                    deps=[e3, se["lx1"][tt]])
        se["f1"][tt] = f1
        f2 = pr.dve(lambda e, z3v=z3v: e.tensor_reduce(tke[:, 5:6], z3v[:], AX, ALU.add),
                    deps=[f1])
        f3 = pr.act(lambda e, z3v=z3v: e.activation(sqe_[:], z3v[:], AF.Square,
                                                    accum_out=tke[:, 6:7]), deps=[f1, e2_])
        se["f3"][tt] = f3
        f4 = pr.dve(lambda e: e.tensor_scalar(tke[:, 5:6], tke[:, 5:6], 1.0 / D, None, ALU.mult),
                    deps=[f2])
        f5 = pr.dve(lambda e: e.tensor_scalar(tke[:, 6:7], tke[:, 6:7], 1.0 / D, None, ALU.mult),
                    deps=[f3, f4])
        f6 = pr.dve(lambda e: e.tensor_tensor(tke[:, 7:8], tke[:, 5:6], tke[:, 5:6], ALU.mult),
                    deps=[f5])
        f7 = pr.dve(lambda e: e.tensor_tensor(tke[:, 6:7], tke[:, 6:7], tke[:, 7:8], ALU.subtract),
                    deps=[f6])
        f8 = pr.act(lambda e: e.activation(tke[:, 6:7], tke[:, 6:7], AF.Ln, bias=cvals[:, 0:1]),
                    deps=[f7])
        f9 = pr.act(lambda e: e.activation(tke[:, 6:7], tke[:, 6:7], AF.Exp, scale=-0.5), deps=[f8])
        fa = pr.dve(lambda e, z3v=z3v: e.tensor_scalar(obuf[:], z3v[:], tke[:, 5:6], tke[:, 6:7],
                                                       ALU.subtract, ALU.mult),
                    deps=[f9, f3, se["od"][tt - 1] if tt else None])
        od = pr.dma(lambda e, tsl=tsl: e.dma_start(out[tsl, :], obuf[:]), deps=[fa])
        se["fa"][tt] = fa
        se["od"][tt] = od
        fin.append(od)

    stage1epe(0)
    stage1edve(0)
    for tt in range(1, NTT):
        stage1epe(tt)
        stage2e(tt - 1)
        stage1edve(tt)
    stage2e(NTT - 1)
    pr.close("e")
    pr.final_wait(fin)

    return dict(dt=dt, out=out, dbg=dbg)



def host_pack(inputs, b, L=4096):
    ii = inputs
    import ml_dtypes
    x = np.asarray(ii["L0_attn_x"][b], dtype=np.float32)
    d = {}
    d["xt"] = np.ascontiguousarray(x[:L].T)
    pp = np.zeros((128, 64), np.float32)
    for i, nm in enumerate(["L0_attn_phazor_init_re", "L0_attn_phazor_init_im",
                            "L0_attn_phazor_angle", "L0_attn_ln_minus_ln_phazor_abs",
                            "L0_attn_last_conv_re", "L0_attn_last_conv_im",
                            "L0_attn_ln_phazor_scale"]):
        pp[:, i * 8:(i + 1) * 8] = np.asarray(ii[nm], np.float32).reshape(8, 128).T
    pp[:, 56:64] = np.asarray(ii["L0_ffn_gate_b"], np.float32).reshape(8, 128).T
    d["pp"] = pp
    nrow = np.concatenate([np.asarray(ii[k], np.float32) for k in
                           ["L0_ffn_n1_g", "L0_ffn_n1_b", "L0_ffn_n2_g", "L0_ffn_n2_b"]])
    d["nrep"] = np.broadcast_to(nrow, (128, 4 * D)).copy()
    for nm, key in [("wq", "L0_ffn_q_w"), ("wk", "L0_ffn_k_w"), ("wv", "L0_ffn_v_w")]:
        w = np.asarray(ii[key], np.float32)
        d[nm] = np.ascontiguousarray(w.reshape(1024, 8, 128).transpose(1, 0, 2))
    d["wgate"] = np.asarray(ii["L0_ffn_gate_w"], np.float32)
    d["wout"] = np.asarray(ii["L0_ffn_out_w"], np.float32)
    sw = np.asarray(ii["L0_ffn_scale_w"], np.float32)
    swp = np.zeros((8, 128, 2), np.float32)
    for hp in range(8):
        swp[hp, 0:64, 0] = sw[2 * hp]
        swp[hp, 64:128, 1] = sw[2 * hp + 1]
    d["swp"] = swp
    d["mgw"] = np.asarray(ii["L0_ffn_moe_gate_w"], np.float32)
    d["mgb_rep"] = np.broadcast_to(np.asarray(ii["L0_ffn_moe_gate_b"], np.float32), (128, 8)).copy()
    d["rsc_rep"] = np.broadcast_to(np.asarray(ii["L0_ffn_res_scale"], np.float32), (128, 8)).copy()
    d["w1"] = np.asarray(ii["L0_ffn_w1"], np.float32)
    d["w2"] = np.asarray(ii["L0_ffn_w2"], np.float32)
    for nm, key in [("b1p", "L0_ffn_b1"), ("lgp", "L0_ffn_ln_g"), ("lbp", "L0_ffn_ln_b")]:
        w = np.asarray(ii[key], np.float32)
        d[nm] = np.ascontiguousarray(w.reshape(8, 16, 128).transpose(0, 2, 1))
    d["b2"] = np.asarray(ii["L0_ffn_b2"], np.float32)
    d["identb"] = np.eye(128, dtype=ml_dtypes.bfloat16)
    d["identf"] = np.eye(128, dtype=np.float32)
    d["iota8"] = np.broadcast_to(np.arange(8, dtype=np.float32), (128, 8)).copy()
    d["onesc_b"] = np.ones((128, 1), ml_dtypes.bfloat16)
    d["onesr_f"] = np.ones((1, 128), np.float32)
    bc2 = np.zeros((2, 128), np.float32)
    bc2[0, 0:64] = 1.0
    bc2[1, 64:128] = 1.0
    d["bc2"] = bc2
    so = np.zeros((128, 2), ml_dtypes.bfloat16)
    so[0:64, 0] = 1.0
    so[64:128, 1] = 1.0
    d["swones_b"] = so
    eoh = np.zeros((8, 8 * 128), ml_dtypes.bfloat16)
    for e in range(8):
        eoh[e, e * 128:(e + 1) * 128] = 1.0
    d["eoh"] = eoh
    return d


def kernel(**inputs):
    nc = bass.Bass("TRN2", target_bir_lowering=False)
    build(nc)
    nc.finalize()
    in_maps = [host_pack(inputs, b) for b in range(8)]
    res = run_bass_kernel_spmd(nc, in_maps, core_ids=list(range(8)))
    return np.stack([np.asarray(r["out"]) for r in res.results], axis=0)


# version 17 (final)
# speedup vs baseline: 1.2030x; 1.2030x over previous
import numpy as np
import concourse.bass as bass
import concourse.mybir as mybir
from concourse.bass_utils import run_bass_kernel_spmd

F32 = mybir.dt.float32
BF16 = mybir.dt.bfloat16
AF = mybir.ActivationFunctionType
ALU = mybir.AluOpType
AX = mybir.AxisListType.X

D = 1024
NG = 8
NE = 8
TWO_PI = float(2.0 * np.pi)
RND_MAGIC = 12582912.0

COMPUTE = ("pe", "dve", "act", "pool")
N_HD = 16
N_GD = 8


class Prog:

    def __init__(self, nc):
        self.nc = nc
        self.sem = {c: nc.alloc_semaphore("sem_" + c) for c in COMPUTE}
        for i in range(N_HD):
            self.sem[("hd", i)] = nc.alloc_semaphore(f"sem_hd{i}")
        for i in range(N_GD):
            self.sem[("gd", i)] = nc.alloc_semaphore(f"sem_gd{i}")
        self.count = {c: 0 for c in self.sem}
        self.meta = []
        self.cur = None
        self.rr = {"hd": 0, "gd": 0}

    def _emit(self, chan, eng, fn, deps, inc):
        assert self.cur is not None, "emit outside phase"
        waits = []
        lw = self.cur["lw"][eng]
        for d in deps:
            if d is None:
                continue
            dchan, dcount = self.meta[d]
            if lw.get(dchan, 0) < dcount:
                waits.append((dchan, dcount))
                lw[dchan] = dcount
        self.count[chan] += inc
        self.cur["ops"][eng].append((fn, chan, inc, waits))
        self.meta.append((chan, self.count[chan]))
        return len(self.meta) - 1

    def pe(self, fn, deps=()):
        return self._emit("pe", "pe", fn, deps, 1)

    def dve(self, fn, deps=()):
        return self._emit("dve", "dve", fn, deps, 1)

    def act(self, fn, deps=()):
        return self._emit("act", "act", fn, deps, 1)

    def pool(self, fn, deps=()):
        return self._emit("pool", "pool", fn, deps, 1)

    def _dma(self, pool, eng, fn, deps):
        i = self.rr[pool]
        self.rr[pool] = (i + 1) % (N_HD if pool == "hd" else N_GD)
        chan = (pool, i)
        lw = self.cur["lw"][eng]
        waits = []
        prev = self.count[chan]
        if prev and lw.get(chan, 0) < prev:
            waits.append((chan, prev))
            lw[chan] = prev
        for d in deps:
            if d is None:
                continue
            dchan, dcount = self.meta[d]
            if lw.get(dchan, 0) < dcount:
                waits.append((dchan, dcount))
                lw[dchan] = dcount
        self.count[chan] += 16
        self.cur["ops"][eng].append((fn, chan, 16, waits))
        self.meta.append((chan, self.count[chan]))
        return len(self.meta) - 1

    def dma(self, fn, deps=()):
        return self._dma("hd", "sp", fn, deps)

    def dmag(self, fn, deps=()):
        return self._dma("gd", "pool", fn, deps)

    def phase(self):
        old = self.cur
        lw = {}
        for eng in ("pe", "dve", "act", "pool", "sp"):
            lw[eng] = {c: self.count[c] for c in COMPUTE}
            if old is not None:
                for c, v in old["lw"][eng].items():
                    if c not in COMPUTE:
                        lw[eng][c] = v
        self.cur = {"ops": {e: [] for e in ("pe", "dve", "act", "pool", "sp")}, "lw": lw}

    def close(self, name=None):
        nc, sem = self.nc, self.sem
        ops = self.cur["ops"]
        with nc.Block(name) as blk:
            def mk(engname):
                def section(engine):
                    for fn, chan, inc, waits in ops[engname]:
                        for wchan, wcount in waits:
                            engine.wait_ge(sem[wchan], wcount)
                        fn(engine).then_inc(sem[chan], inc)
                return section
            if ops["sp"]:
                blk.sync(mk("sp"))
            if ops["pe"]:
                blk.tensor(mk("pe"))
            if ops["dve"]:
                blk.vector(mk("dve"))
            if ops["act"]:
                blk.scalar(mk("act"))
            if ops["pool"]:
                blk.gpsimd(mk("pool"))
        self.phase()

    def final_wait(self, deps):
        s = self
        with self.nc.Block("final") as blk:
            def section(engine):
                for d in deps:
                    dchan, dcount = s.meta[d]
                    engine.wait_ge(s.sem[dchan], dcount)
            blk.sync(section)


def build(nc, L=4096, C=1024, debug=()):
    NCH = L // C
    NTC = L // 512

    dt = {}
    def din(name, shape, dtype=F32):
        dt[name] = nc.dram_tensor(name, list(shape), dtype, kind="ExternalInput")
        return dt[name]

    din("xt", (D, L))
    din("pp", (128, 64))
    din("nrep", (128, 4 * D))
    din("wq", (NG, D, 128)); din("wk", (NG, D, 128)); din("wv", (NG, D, 128))
    din("wgate", (D, D)); din("wout", (D, D))
    din("swp", (NG, 128, 2))
    din("mgw", (D, NE)); din("mgb_rep", (128, NE)); din("rsc_rep", (128, NE))
    din("w1", (NE, D, 2048)); din("w2", (NE, 2048, D))
    din("b1p", (NE, 128, 16)); din("lgp", (NE, 128, 16)); din("lbp", (NE, 128, 16))
    din("b2", (NE, D))
    din("identb", (128, 128), BF16)
    din("identf", (128, 128))
    din("iota8", (128, NE))
    din("onesc_b", (128, 1), BF16)
    din("onesr_f", (1, 128))
    din("bc2", (2, 128))
    din("swones_b", (128, 2), BF16)
    din("eoh", (NE, NE * 128), BF16)

    out = nc.dram_tensor("out", [L, D], F32, kind="ExternalOutput")
    yacc = nc.dram_tensor("yacc", [L, D], F32, kind="Internal")
    x1tok = nc.dram_tensor("x1tok", [L, D], BF16, kind="Internal")
    dbg = {}
    for name in debug:
        shp = {"dz": (D, L), "dx1": (D, L), "dx2": (D, L), "dmwr": (8, L), "doa": (D, L), "dtk": (128, 16 * (L // 128)), "dlg": (128, 8 * (L // 128)), "dxf": (128, 1024), "dps": (128, 8)}[name]
        dbg[name] = nc.dram_tensor(name, list(shp), F32, kind="ExternalOutput")

    SB_BYTES = 205 * 1024
    big = nc.alloc_sbuf_tensor("big", [128, SB_BYTES], mybir.dt.uint8, side="left")

    def sb(name, shape, dtype=F32, at=None):
        nbytes = int(np.prod(shape[1:])) * mybir.dt.size(dtype)
        assert at is not None and at % 4 == 0
        assert at + nbytes <= SB_BYTES, f"SBUF overflow: {name} ends {at + nbytes}"
        view = big[0:shape[0], at:at + nbytes].bitcast(dtype)
        assert tuple(view.shape) == tuple(shape), (name, view.shape, shape)
        return view

    o = 0
    def nxt(n):
        nonlocal o
        r = o
        o = r + ((n + 31) // 32) * 32
        return r

    pp = sb("pp_sb", (128, 64), at=nxt(256))
    identb = sb("identb_sb", (128, 128), BF16, at=nxt(256))
    identf = sb("identf_sb", (128, 128), at=nxt(512))
    iota8 = sb("iota8_sb", (128, NE), at=nxt(32))
    onesc_b = sb("onesc_sb", (128, 1), BF16, at=nxt(32))
    onesr_f = sb("onesr_sb", (1, 128), at=nxt(512))
    bc2 = sb("bc2_sb", (2, 128), at=nxt(512))
    swones_b = sb("swones_sb", (128, 2), BF16, at=nxt(32))
    eoh = sb("eoh_sb", (NE, NE * 128), BF16, at=nxt(2048))
    mgb = sb("mgb_sb", (128, NE), at=nxt(32))
    rscr = sb("rsc_sb", (128, NE), at=nxt(32))
    dparams = sb("dparams", (128, 128), at=nxt(512))
    pscr = sb("pscr", (128, 256), at=nxt(1024))
    ek = sb("ek", (128, 2 * 12), at=nxt(96))
    kc = sb("kc", (128, 8), at=nxt(32))
    cvals = sb("cvals", (128, 4), at=nxt(32))
    PERSIST_END = o
    assert PERSIST_END <= 8 * 1024, PERSIST_END

    Z_OFF = 16 * 1024
    z = sb("z", (128, NG * L), F32, at=Z_OFF)
    x1b = [sb(f"x1b{g}", (128, L), BF16, at=Z_OFF + g * L * 4) for g in range(NG)]
    SCRATCH = Z_OFF + NG * L * 4
    ssm_names = ["Cc", "Cs", "aux", "aux2", "cct", "sst", "gre", "gim", "rhot"]
    ssm = {}
    for i, nm in enumerate(ssm_names):
        ssm[nm] = sb(nm, (128, C), F32, at=SCRATCH + i * C * 4)
    zbch = sb("zbch", (128, 512), BF16, at=SCRATCH)
    sqch = sb("sqch", (128, 512), BF16, at=SCRATCH + 1024)
    tsc = sb("tsc", (128, 512), F32, at=SCRATCH + 2048)
    r0 = sb("r0", (1, 512), at=SCRATCH + 4096)
    r1 = sb("r1", (1, 512), at=SCRATCH + 6144)
    r2 = sb("r2", (1, 512), at=SCRATCH + 8192)
    nbc = sb("nbc", (128, 4 * D), at=SCRATCH + 10240)

    pbig = nc.alloc_psum_tensor("pbig", [128, 2048], F32)
    pb4 = nc.alloc_psum_tensor("pb4", [128, 512], F32)
    pb5 = nc.alloc_psum_tensor("pb5", [128, 512], F32)
    pb7 = nc.alloc_psum_tensor("pb7", [128, 512], F32)
    pbt = nc.alloc_psum_tensor("pbt", [128, 1024], BF16)

    pr = Prog(nc)
    pr.phase()

    ld = {}
    for nm, t in [("pp", pp), ("identb", identb), ("identf", identf), ("iota8", iota8),
                  ("onesc_b", onesc_b), ("onesr_f", onesr_f), ("bc2", bc2),
                  ("swones_b", swones_b), ("eoh", eoh)]:
        ld[nm] = pr.dma(lambda e, nm=nm, t=t: e.dma_start(t[:], dt[nm][:]))
    ld["mgb"] = pr.dma(lambda e: e.dma_start(mgb[:], dt["mgb_rep"][:]))
    ld["rsc"] = pr.dma(lambda e: e.dma_start(rscr[:], dt["rsc_rep"][:]))
    ldx = [pr.dma(lambda e, g=g: e.dma_start(z[:, g * L:(g + 1) * L],
                                             dt["xt"][g * 128:(g + 1) * 128, :]))
           for g in range(NG)]

    P_INIT_RE, P_INIT_IM, P_ANGLE, P_LNM, P_LC_RE, P_LC_IM, P_LPS, P_GATEB = range(8)
    RHO, GATE, COSTH, SINTH, GI_RE, GI_IM, GIN_RE, GIN_IMN = range(8)
    GI_REN = 8
    ppc = lambda i: pp[:, i * 8:(i + 1) * 8]
    dpc = lambda i: dparams[:, i * 8:(i + 1) * 8]
    scr = lambda i: pscr[:, i * 8:(i + 1) * 8]

    c_eps = pr.dve(lambda e: e.memset(cvals[:, 0:1], 1e-5))
    dep = ld["pp"]
    o1 = pr.act(lambda e: e.activation(scr(0), ppc(P_LNM), AF.Exp), deps=[dep])
    o2 = pr.dve(lambda e: e.tensor_tensor(scr(1), scr(0), ppc(P_LPS), ALU.mult), deps=[o1])
    o3 = pr.act(lambda e: e.activation(dpc(RHO), scr(1), AF.Exp, scale=-1.0), deps=[o2])
    o4 = pr.dve(lambda e: e.tensor_scalar(dpc(GATE), dpc(RHO), -1.0, 1.0, ALU.mult, ALU.add), deps=[o3])
    t1 = pr.dve(lambda e: e.tensor_tensor(scr(2), ppc(P_ANGLE), ppc(P_LPS), ALU.mult), deps=[dep])
    t2 = pr.dve(lambda e: e.tensor_scalar(scr(3), scr(2), 1.0 / TWO_PI, RND_MAGIC, ALU.mult, ALU.add), deps=[t1])
    t3 = pr.dve(lambda e: e.tensor_scalar(scr(4), scr(3), RND_MAGIC, -TWO_PI, ALU.subtract, ALU.mult), deps=[t2])
    t4 = pr.dve(lambda e: e.tensor_tensor(scr(5), scr(2), scr(4), ALU.add), deps=[t3])
    u1 = pr.act(lambda e: e.activation(scr(6), scr(5), AF.Sin, scale=0.5), deps=[t4])
    u2 = pr.act(lambda e: e.activation(scr(7), scr(6), AF.Square), deps=[u1])
    u3 = pr.dve(lambda e: e.tensor_scalar(dpc(COSTH), scr(7), -2.0, 1.0, ALU.mult, ALU.add), deps=[u2])
    u4 = pr.act(lambda e: e.activation(scr(8), scr(5), AF.Sin, scale=0.25), deps=[t4])
    u5 = pr.act(lambda e: e.activation(scr(9), scr(8), AF.Square), deps=[u4])
    u6 = pr.dve(lambda e: e.tensor_scalar(scr(10), scr(9), -2.0, 1.0, ALU.mult, ALU.add), deps=[u5])
    u7 = pr.dve(lambda e: e.scalar_tensor_tensor(dpc(SINTH), scr(6), 2.0, scr(10), ALU.mult, ALU.mult),
                deps=[u6, u1])
    g1 = pr.dve(lambda e: e.tensor_tensor(dpc(GI_RE), dpc(GATE), ppc(P_INIT_RE), ALU.mult), deps=[o4])
    g2 = pr.dve(lambda e: e.tensor_tensor(dpc(GI_IM), dpc(GATE), ppc(P_INIT_IM), ALU.mult), deps=[o4])
    g3 = pr.dve(lambda e: e.tensor_scalar(dpc(GI_REN), dpc(GI_RE), -1.0, None, ALU.mult), deps=[g1])
    q1 = pr.act(lambda e: e.activation(scr(11), ppc(P_INIT_RE), AF.Square), deps=[dep])
    q2 = pr.act(lambda e: e.activation(scr(12), ppc(P_INIT_IM), AF.Square), deps=[dep])
    q3 = pr.dve(lambda e: e.tensor_tensor(scr(13), scr(11), scr(12), ALU.add), deps=[q1, q2])
    q4 = pr.dve(lambda e: e.reciprocal(scr(14), scr(13)), deps=[q3])
    q5 = pr.dve(lambda e: e.tensor_tensor(scr(15), ppc(P_LC_RE), ppc(P_INIT_RE), ALU.mult), deps=[dep])
    q6 = pr.dve(lambda e: e.tensor_tensor(scr(16), ppc(P_LC_IM), ppc(P_INIT_IM), ALU.mult), deps=[dep])
    q7 = pr.dve(lambda e: e.tensor_tensor(scr(17), scr(15), scr(16), ALU.add), deps=[q5, q6])
    q8 = pr.dve(lambda e: e.tensor_tensor(scr(18), scr(17), scr(14), ALU.mult), deps=[q7, q4])
    q9 = pr.dve(lambda e: e.tensor_tensor(scr(19), ppc(P_LC_IM), ppc(P_INIT_RE), ALU.mult), deps=[dep])
    qa = pr.dve(lambda e: e.tensor_tensor(scr(20), ppc(P_LC_RE), ppc(P_INIT_IM), ALU.mult), deps=[dep])
    qb = pr.dve(lambda e: e.tensor_tensor(scr(21), scr(19), scr(20), ALU.subtract), deps=[q9, qa])
    qc = pr.dve(lambda e: e.tensor_tensor(scr(22), scr(21), scr(14), ALU.mult), deps=[qb, q4])
    r1_ = pr.dve(lambda e: e.tensor_tensor(scr(23), scr(18), dpc(COSTH), ALU.mult), deps=[q8, u3])
    r2_ = pr.dve(lambda e: e.tensor_tensor(scr(24), scr(22), dpc(SINTH), ALU.mult), deps=[qc, u7])
    r3_ = pr.dve(lambda e: e.tensor_tensor(dpc(GIN_RE), scr(23), scr(24), ALU.subtract), deps=[r1_, r2_])
    r4_ = pr.dve(lambda e: e.tensor_tensor(scr(25), scr(18), dpc(SINTH), ALU.mult), deps=[q8, u7])
    r5_ = pr.dve(lambda e: e.tensor_tensor(scr(26), scr(22), dpc(COSTH), ALU.mult), deps=[qc, u3])
    r6_ = pr.dve(lambda e: e.tensor_tensor(scr(27), scr(25), scr(26), ALU.add), deps=[r4_, r5_])
    r7_ = pr.dve(lambda e: e.tensor_scalar(dpc(GIN_IMN), scr(27), -1.0, None, ALU.mult), deps=[r6_])
    params_done = [o4, u3, u7, g1, g2, g3, r3_, r7_]
    pr.close("p0")

    Cc, Cs, aux, aux2, cct, sst, gre, gim, rhot = (ssm[n] for n in ssm_names)
    nlad = int(np.log2(C))
    z_done = []
    for g in range(NG):
        P = lambda view: view[:, g:g + 1]
        zg = z[:, g * L:(g + 1) * L]
        d0 = pr.dve(lambda e: e.memset(Cc[:, 0:1], 1.0), deps=params_done if g == 0 else z_done[-1:])
        d1 = pr.dve(lambda e: e.memset(Cs[:, 0:1], 0.0), deps=[d0])
        e0 = pr.dve(lambda e: e.tensor_copy(ek[:, 0:1], P(dpc(COSTH))), deps=[d1])
        e1 = pr.dve(lambda e: e.tensor_copy(ek[:, 1:2], P(dpc(SINTH))), deps=[e0])
        prev = e1
        for k in range(nlad):
            w = 1 << k
            ekre = ek[:, 2 * k:2 * k + 1]
            ekim = ek[:, 2 * k + 1:2 * k + 2]
            ng1 = pr.dve(lambda e, ekim=ekim: e.tensor_scalar(kc[:, 2:3], ekim, -1.0, None, ALU.mult),
                         deps=[prev])
            a1 = pr.dve(lambda e, w=w, ekre=ekre: e.tensor_scalar(aux[:, 0:w], Cc[:, 0:w], ekre, None,
                                                                  ALU.mult), deps=[ng1])
            a2 = pr.dve(lambda e, w=w: e.scalar_tensor_tensor(Cc[:, w:2 * w], Cs[:, 0:w], kc[:, 2:3],
                                                              aux[:, 0:w], ALU.mult, ALU.add), deps=[a1])
            a3 = pr.dve(lambda e, w=w, ekre=ekre: e.tensor_scalar(aux2[:, 0:w], Cs[:, 0:w], ekre, None,
                                                                  ALU.mult), deps=[a2])
            a4 = pr.dve(lambda e, w=w, ekim=ekim: e.scalar_tensor_tensor(Cs[:, w:2 * w], Cc[:, 0:w], ekim,
                                                                         aux2[:, 0:w], ALU.mult, ALU.add),
                        deps=[a3])
            b1 = pr.dve(lambda e, ekre=ekre: e.tensor_tensor(kc[:, 3:4], ekre, ekre, ALU.mult), deps=[a4])
            b2 = pr.dve(lambda e, ekim=ekim: e.tensor_tensor(kc[:, 4:5], ekim, ekim, ALU.mult), deps=[b1])
            b3 = pr.dve(lambda e, ekre=ekre, ekim=ekim: e.tensor_tensor(kc[:, 5:6], ekre, ekim, ALU.mult),
                        deps=[b2])
            b4 = pr.dve(lambda e, k=k: e.tensor_tensor(ek[:, 2 * k + 2:2 * k + 3], kc[:, 3:4], kc[:, 4:5],
                                                       ALU.subtract), deps=[b3])
            b5 = pr.dve(lambda e, k=k: e.tensor_scalar(ek[:, 2 * k + 3:2 * k + 4], kc[:, 5:6], 2.0, None,
                                                       ALU.mult), deps=[b4])
            prev = b5
        ECre = ek[:, 2 * nlad:2 * nlad + 1]
        ECim = ek[:, 2 * nlad + 1:2 * nlad + 2]
        rt = pr.dve(lambda e: e.tensor_scalar(rhot[:], Cc[:], 0.0, P(dpc(RHO)), ALU.mult, ALU.add),
                    deps=[prev])
        k0 = pr.dve(lambda e: e.memset(kc[:, 0:1], 1.0), deps=[rt])
        k1 = pr.dve(lambda e: e.memset(kc[:, 1:2], 0.0), deps=[k0])
        prev = k1
        carry_re, carry_im = P(dpc(GIN_RE)), P(dpc(GIN_IMN))
        for c in range(NCH):
            xgc = zg[:, c * C:(c + 1) * C]
            j0 = pr.dve(lambda e: e.tensor_scalar(kc[:, 2:3], kc[:, 1:2], -1.0, None, ALU.mult), deps=[prev])
            c1 = pr.act(lambda e: e.activation(aux[:], Cc[:], AF.Copy, scale=kc[:, 0:1]), deps=[j0])
            c2 = pr.dve(lambda e: e.scalar_tensor_tensor(cct[:], Cs[:], kc[:, 2:3], aux[:], ALU.mult,
                                                         ALU.add), deps=[c1])
            c3 = pr.act(lambda e: e.activation(aux2[:], Cs[:], AF.Copy, scale=kc[:, 0:1]), deps=[c2])
            c4 = pr.dve(lambda e: e.scalar_tensor_tensor(sst[:], Cc[:], kc[:, 1:2], aux2[:], ALU.mult,
                                                         ALU.add), deps=[c3])
            i1 = pr.dve(lambda e, xgc=xgc: e.tensor_tensor(aux[:], xgc, cct[:], ALU.mult),
                        deps=[c4, ldx[g]])
            i2 = pr.dve(lambda e, xgc=xgc: e.tensor_tensor(aux2[:], xgc, sst[:], ALU.mult), deps=[i1])
            sc1 = pr.dve(lambda e, cr=carry_re: e.tensor_tensor_scan(gre[:], rhot[:], aux[:], cr,
                                                                     ALU.mult, ALU.add), deps=[i2])
            sc2 = pr.dve(lambda e, ci=carry_im: e.tensor_tensor_scan(gim[:], rhot[:], aux2[:], ci,
                                                                     ALU.mult, ALU.add), deps=[sc1])
            p1 = pr.act(lambda e: e.activation(aux[:], gre[:], AF.Copy, scale=P(dpc(GI_RE))), deps=[sc2])
            p2 = pr.dve(lambda e: e.scalar_tensor_tensor(aux[:], gim[:], P(dpc(GI_IM)), aux[:], ALU.mult,
                                                         ALU.add), deps=[p1])
            p3 = pr.act(lambda e: e.activation(aux2[:], gre[:], AF.Copy, scale=P(dpc(GI_IM))), deps=[p2])
            p4 = pr.dve(lambda e: e.scalar_tensor_tensor(aux2[:], gim[:], P(dpc(GI_REN)), aux2[:], ALU.mult,
                                                         ALU.add), deps=[p3])
            p5 = pr.dve(lambda e: e.tensor_tensor(cct[:], aux[:], cct[:], ALU.mult), deps=[p4])
            p6 = pr.dve(lambda e: e.tensor_tensor(sst[:], aux2[:], sst[:], ALU.mult), deps=[p5])
            p7 = pr.dve(lambda e: e.tensor_tensor(cct[:], cct[:], sst[:], ALU.subtract), deps=[p6])
            p8 = pr.dve(lambda e, xgc=xgc: e.tensor_tensor(xgc, xgc, cct[:], ALU.add), deps=[p7])
            prev = p8
            carry_re = gre[:, C - 1:C]
            carry_im = gim[:, C - 1:C]
            if c < NCH - 1:
                k2 = pr.dve(lambda e: e.tensor_tensor(kc[:, 6:7], kc[:, 0:1], ECre, ALU.mult), deps=[prev])
                k3 = pr.dve(lambda e: e.tensor_tensor(kc[:, 7:8], kc[:, 1:2], ECim, ALU.mult), deps=[k2])
                k4 = pr.dve(lambda e: e.tensor_tensor(kc[:, 6:7], kc[:, 6:7], kc[:, 7:8], ALU.subtract),
                            deps=[k3])
                k5 = pr.dve(lambda e: e.tensor_tensor(kc[:, 7:8], kc[:, 0:1], ECim, ALU.mult), deps=[k4])
                k6 = pr.dve(lambda e: e.scalar_tensor_tensor(kc[:, 1:2], kc[:, 1:2], ECre, kc[:, 7:8],
                                                             ALU.mult, ALU.add), deps=[k5])
                k7 = pr.dve(lambda e: e.tensor_copy(kc[:, 0:1], kc[:, 6:7]), deps=[k6])
                prev = k7
        z_done.append(prev)
        pr.close(f"ssm_g{g}")

    z_readers = []
    if "dz" in dbg:
        z_readers = [pr.dma(lambda e, g=g: e.dma_start(dbg["dz"][g * 128:(g + 1) * 128, :],
                                                       z[:, g * L:(g + 1) * L]), deps=[z_done[g]])
                     for g in range(NG)]
        pr.close("dbg_z")

    x1_done = [None] * NG
    pv0 = pv1 = None
    for tch in range(NTC):
        csl = slice(tch * 512, (tch + 1) * 512)
        mmm = None
        mms = None
        sq = None
        prev_cp = [z_done[-1]] if tch == 0 else []
        for g in range(NG):
            cp = pr.act(lambda e, g=g, tch=tch: e.activation(
                zbch[:], z[:, g * L + tch * 512:g * L + (tch + 1) * 512], AF.Copy),
                deps=prev_cp + [mmm, sq])
            sq = pr.act(lambda e: e.activation(sqch[:], zbch[:], AF.Square),
                        deps=[cp, mms])
            mmm = pr.pe(lambda e, g=g: e.matmul(pb4[0:1, :], onesc_b[:], zbch[:],
                                                start=(g == 0), stop=(g == NG - 1)),
                        deps=[cp, ld["onesc_b"]] + ([pv0] if g == 0 else []))
            mms = pr.pe(lambda e, g=g: e.matmul(pb5[0:1, :], onesc_b[:], sqch[:],
                                                start=(g == 0), stop=(g == NG - 1)),
                        deps=[sq] + ([pv1] if g == 0 else []))
            prev_cp = [mmm]
        v0 = pr.dve(lambda e: e.tensor_scalar(r0[:], pb4[0:1, :], 1.0 / D, None, ALU.mult), deps=[mmm])
        v1 = pr.dve(lambda e: e.tensor_scalar(r1[:], pb5[0:1, :], 1.0 / D, None, ALU.mult), deps=[mms, v0])
        pv0, pv1 = v0, v1
        v2 = pr.dve(lambda e: e.tensor_tensor(r2[:], r0[:], r0[:], ALU.mult), deps=[v1])
        v3 = pr.dve(lambda e: e.tensor_tensor(r1[:], r1[:], r2[:], ALU.subtract), deps=[v2])
        v4 = pr.act(lambda e: e.activation(r1[:], r1[:], AF.Ln, bias=cvals[0:1, 0:1]), deps=[v3])
        v5 = pr.act(lambda e: e.activation(r1[:], r1[:], AF.Exp, scale=-0.5), deps=[v4])
        bm = pr.pe(lambda e: e.matmul(pbig[:, 0:512], onesr_f[:], r0[:], start=True, stop=True),
                   deps=[v2, ld["onesr_f"], x1_done[-1] if x1_done else None])
        br = pr.pe(lambda e: e.matmul(pbig[:, 512:1024], onesr_f[:], r1[:], start=True, stop=True), deps=[v5, bm])
        ap_prev = [br]
        for g in range(NG):
            s1 = pr.dve(lambda e, g=g, tch=tch: e.tensor_tensor(
                tsc[:], z[:, g * L + tch * 512:g * L + (tch + 1) * 512],
                pbig[:, 0:512], ALU.subtract), deps=ap_prev + [bm])
            s2 = pr.dve(lambda e, g=g, csl=csl: e.tensor_tensor(x1b[g][:, csl], tsc[:],
                                                                pbig[:, 512:1024], ALU.mult),
                        deps=[s1, br] + z_readers)
            ap_prev = [s2]
        x1_done.append(ap_prev[0])
        x1_done = x1_done[-NG:] if False else x1_done
    x1_fin = x1_done[-1]
    pr.close("ln0")

    dbg_last = None
    if "dx1" in dbg:
        prevd = None
        for g in range(NG):
            for tch in range(NTC):
                cc = pr.dve(lambda e, g=g, tch=tch: e.tensor_copy(
                    tsc[:], x1b[g][:, tch * 512:(tch + 1) * 512]), deps=[x1_fin, prevd])
                prevd = pr.dma(lambda e, g=g, tch=tch: e.dma_start(
                    dbg["dx1"][g * 128:(g + 1) * 128, tch * 512:(tch + 1) * 512], tsc[:]), deps=[cc])
        dbg_last = prevd
        pr.close("dbg_x1")

    NTT = L // 128
    SCR = SCRATCH
    qp = sb("qp", (128, L), BF16, at=SCR)
    kp = sb("kp", (128, L), BF16, at=SCR + L * 2)
    vt = sb("vt", (128, NTT * 130), BF16, at=SCR + L * 4)
    VT_END = SCR + L * 4 + ((NTT * 130 * 2 + 31) // 32) * 32
    prb = sb("prb", (128, 4096), BF16, at=VT_END + 24 * 1024)
    wqp2 = [sb(f"wqp{i}", (128, 1024), BF16, at=VT_END + 4096 + i * 13568) for i in range(2)]
    wkp2 = [sb(f"wkp{i}", (128, 1024), BF16, at=VT_END + 6144 + i * 13568) for i in range(2)]
    wvp2 = [sb(f"wvp{i}", (128, 1024), BF16, at=VT_END + 8192 + i * 13568) for i in range(2)]
    swpb2 = [sb(f"swpb{i}", (128, 2), BF16, at=VT_END + 10240 + i * 13568) for i in range(2)]
    rsch = sb("rsch", (2, 512), F32, at=VT_END + 10304)
    sqc = sb("sqc", (128, 512), BF16, at=VT_END + 12352)
    qnm = sb("qnm", (2, 16), F32, at=VT_END + 13376)
    rbs = sb("rbs", (64, 1024), F32, at=VT_END + 13440)
    negc = sb("negc", (128, 2), F32, at=VT_END + 17536)
    crow = sb("crow", (1, 2), F32, at=VT_END + 17600)

    ldw = {}
    out_done = [[None] * NTC for _ in range(NG)]
    x1_users = [x1_fin]
    oa_off = [Z_OFF + g * L * 4 + L * 2 for g in range(NG)]
    out_all = [sb(f"oa{g}", (128, L), BF16, at=oa_off[g]) for g in range(NG)]

    vt_ones = None
    prev_pair_ops = [x1_fin, dbg_last]
    wbuf_users = [[x1_fin, dbg_last], [x1_fin, dbg_last]]
    wload = {}
    for hp in range(NG):
        pr2 = pr
        wpar = hp % 2
        wqp, wkp, wvp, swpb = wqp2[wpar], wkp2[wpar], wvp2[wpar], swpb2[wpar]
        if hp in wload:
            lw1, lw2, lw3, lw4 = wload.pop(hp)
        else:
            lw1 = pr.dmag(lambda e, hp=hp, wqp=wqp: e.dma_start(
                wqp[:].rearrange("p (g c) -> p g c", c=128), dt["wq"][hp].rearrange("(g p) c -> p g c", p=128)), deps=wbuf_users[wpar])
            lw2 = pr.dmag(lambda e, hp=hp, wkp=wkp: e.dma_start(
                wkp[:].rearrange("p (g c) -> p g c", c=128), dt["wk"][hp].rearrange("(g p) c -> p g c", p=128)), deps=wbuf_users[wpar])
            lw3 = pr.dmag(lambda e, hp=hp, wvp=wvp: e.dma_start(
                wvp[:].rearrange("p (g c) -> p g c", c=128), dt["wv"][hp].rearrange("(g p) c -> p g c", p=128)), deps=wbuf_users[wpar])
            lw4 = pr.dmag(lambda e, hp=hp, swpb=swpb: e.dma_start(swpb[:], dt["swp"][hp]), deps=wbuf_users[wpar])
        if vt_ones is None:
            m1_ = pr.dve(lambda e: e.memset(vt.rearrange("p (j c) -> p j c", c=130)[:, :, 64], 1.0),
                         deps=prev_pair_ops)
            vt_ones = pr.dve(lambda e: e.memset(vt.rearrange("p (j c) -> p j c", c=130)[:, :, 129], 1.0),
                             deps=[m1_])
        qev = [None] * NTC
        kev = [None] * NTC
        nm0 = pr.dve(lambda e: e.memset(qnm[:, 0:2], 0.0), deps=prev_pair_ops)
        nmdep = [nm0]
        vtr_done = []
        lastq = lastk = lastv = lastrm = None
        for tc in range(NTC):
            csl = slice(tc * 512, (tc + 1) * 512)
            for which, wsb, dst, lww in (("q", wqp, qp, lw1), ("k", wkp, kp, lw2)):
                pv = pbig[:, 0:512] if which == "q" else pbig[:, 512:1024]
                mms = []
                for g in range(NG):
                    mms.append(pr.pe(lambda e, g=g, wsb=wsb, pv=pv, csl=csl: e.matmul(
                        pv, wsb[:, g * 128:(g + 1) * 128], x1b[g][:, csl],
                        start=(g == 0), stop=(g == NG - 1)),
                        deps=([lww, x1_fin] + prev_pair_ops) if g == 0 else []))
                ev = pr.dve(lambda e, dst=dst, pv=pv, csl=csl: e.tensor_copy(dst[:, csl], pv),
                            deps=[mms[-1]])
                if which == "q":
                    qev[tc] = ev
                    lastq = mms[-1]
                else:
                    kev[tc] = ev
                    lastk = mms[-1]
                x1_users.append(mms[0])
            vmm = []
            for g in range(NG):
                vmm.append(pr.pe(lambda e, g=g, csl=csl, wvp=wvp: e.matmul(
                    pbig[:, 1024:1536], wvp[:, g * 128:(g + 1) * 128], x1b[g][:, csl],
                    start=(g == 0), stop=(g == NG - 1)),
                    deps=([lw3, x1_fin] + prev_pair_ops) if g == 0 else []))
            x1_users.append(vmm[0])
            lastv = vmm[-1]
            vevac = pr.dve(lambda e, csl=csl: e.tensor_copy(sqc[:], pbig[:, 1024:1536]), deps=[vmm[-1]])
            for jj in range(4):
                j = tc * 4 + jj
                tp = pr.pe(lambda e, jj=jj, j=j: e.transpose(
                    pbt[:, (j % 8) * 128:(j % 8 + 1) * 128], sqc[:, jj * 128:(jj + 1) * 128], identb[:]),
                    deps=[vevac, ld["identb"]] + vtr_done[-8:-7])
                cpA = pr.dve(lambda e, j=j: e.tensor_copy(
                    vt[:, j * 130:j * 130 + 64], pbt[:, (j % 8) * 128:(j % 8) * 128 + 64]),
                    deps=[tp, vt_ones])
                cpB = pr.dve(lambda e, j=j: e.tensor_copy(
                    vt[:, j * 130 + 65:j * 130 + 129], pbt[:, (j % 8) * 128 + 64:(j % 8 + 1) * 128]),
                    deps=[tp])
                vtr_done.append(cpB)
            rmm = pr.pe(lambda e, csl=csl, swpb=swpb: e.matmul(pb7[0:2, :], swpb[:], qp[:, csl],
                                                              start=True, stop=True),
                        deps=[qev[tc], lw4])
            lastrm = rmm
            rsg = pr.act(lambda e: e.activation(rsch[:], pb7[0:2, :], AF.Sigmoid), deps=[rmm])
            rsc2 = pr.dve(lambda e: e.tensor_scalar(rsch[:], rsch[:], 0.25, None, ALU.mult), deps=[rsg])
            rbc = pr.pe(lambda e: e.matmul(pb7[:], bc2[:], rsch[:], start=True, stop=True),
                        deps=[rsc2, ld["bc2"], rsg])
            qsc = pr.dve(lambda e, csl=csl: e.tensor_tensor(qp[:, csl], qp[:, csl], pb7[:], ALU.mult),
                         deps=[rbc])
            qev[tc] = qsc
            sq1 = pr.act(lambda e, csl=csl: e.activation(sqc[:], qp[:, csl], AF.Square),
                         deps=[qsc, vevac, tp])
            nm1 = pr.pe(lambda e: e.matmul(pb7[0:2, :], swones_b[:], sqc[:], start=True, stop=True),
                        deps=[sq1, rbc, ld["swones_b"]])
            nr1 = pr.dve(lambda e: e.tensor_reduce(qnm[:, 2:3], pb7[0:2, :], AX, ALU.max),
                         deps=[nm1] + nmdep)
            nx1 = pr.dve(lambda e: e.tensor_tensor(qnm[:, 0:1], qnm[:, 0:1], qnm[:, 2:3], ALU.max),
                         deps=[nr1])
            sq2 = pr.act(lambda e, csl=csl: e.activation(sqc[:], kp[:, csl], AF.Square),
                         deps=[kev[tc], nm1])
            nm2 = pr.pe(lambda e: e.matmul(pb7[0:2, :], swones_b[:], sqc[:], start=True, stop=True),
                        deps=[sq2, nr1])
            nr2 = pr.dve(lambda e: e.tensor_reduce(qnm[:, 3:4], pb7[0:2, :], AX, ALU.max),
                         deps=[nm2, nx1])
            nx2 = pr.dve(lambda e: e.tensor_tensor(qnm[:, 1:2], qnm[:, 1:2], qnm[:, 3:4], ALU.max),
                         deps=[nr2])
            nmdep = [nx2]
        wbuf_users[wpar] = [lastq, lastk, lastv, lastrm]
        if hp + 1 < NG:
            npar = (hp + 1) % 2
            nq, nk, nv, nsw = wqp2[npar], wkp2[npar], wvp2[npar], swpb2[npar]
            wload[hp + 1] = (
                pr.dmag(lambda e, hp=hp, nq=nq: e.dma_start(
                    nq[:].rearrange("p (g c) -> p g c", c=128),
                    dt["wq"][hp + 1].rearrange("(g p) c -> p g c", p=128)),
                    deps=wbuf_users[npar]),
                pr.dmag(lambda e, hp=hp, nk=nk: e.dma_start(
                    nk[:].rearrange("p (g c) -> p g c", c=128),
                    dt["wk"][hp + 1].rearrange("(g p) c -> p g c", p=128)),
                    deps=wbuf_users[npar]),
                pr.dmag(lambda e, hp=hp, nv=nv: e.dma_start(
                    nv[:].rearrange("p (g c) -> p g c", c=128),
                    dt["wv"][hp + 1].rearrange("(g p) c -> p g c", p=128)),
                    deps=wbuf_users[npar]),
                pr.dmag(lambda e, hp=hp, nsw=nsw: e.dma_start(nsw[:], dt["swp"][hp + 1]),
                        deps=wbuf_users[npar]),
            )
        cb1 = pr.act(lambda e: e.activation(qnm[:, 4:6], qnm[:, 0:2], AF.Sqrt), deps=nmdep)
        cb2 = pr.dve(lambda e: e.tensor_tensor(qnm[:, 6:7], qnm[:, 4:5], qnm[:, 5:6], ALU.mult),
                     deps=[cb1])
        ctp = pr.pe(lambda e: e.transpose(pbig[0:1, 0:2], qnm[:, 6:7], identf[0:2, 0:2]),
                    deps=[cb2, ld["identf"]])
        ccp = pr.dve(lambda e: e.tensor_copy(crow[:], pbig[0:1, 0:2]), deps=[ctp])
        cmx = pr.dve(lambda e: e.tensor_reduce(crow[0:1, 0:1], crow[:], AX, ALU.max), deps=[ccp])
        cb3a = pr.pe(lambda e: e.matmul(pb7[:, 0:1], onesr_f[:], crow[0:1, 0:1], start=True, stop=True),
                     deps=[cmx, ld["onesr_f"]])
        cb4 = pr.dve(lambda e: e.tensor_scalar(negc[:, 0:1], pb7[:, 0:1], -1.0, None, ALU.mult),
                     deps=[cb3a])
        prev_pair_ops = [cb4]
        avA_last = None
        carry_exps = (None, None)
        carry_avs = (None,) * 4
        pending_evac = None
        def emit_evac(pend):
            nonlocal avA_last
            aA, aB, csl, tc_ = pend
            rc1 = pr.dve(lambda e: e.reciprocal(rbs[0:1, 0:512], pb4[64:65, :]), deps=[aA])
            rc2 = pr.dve(lambda e: e.reciprocal(rbs[0:1, 512:1024], pb5[64:65, :]),
                         deps=[aB, rc1])
            bA = pr.pe(lambda e: e.matmul(pb7[0:64, :], onesr_f[0:1, 0:64],
                                          rbs[0:1, 0:512], start=True, stop=True),
                       deps=[rc1])
            bB = pr.pe(lambda e: e.matmul(pb7[64:128, :], onesr_f[0:1, 0:64],
                                          rbs[0:1, 512:1024], start=True, stop=True), deps=[rc2])
            cA = pr.dve(lambda e: e.tensor_copy(rbs[:, 0:512], pb7[0:64, :]), deps=[bA, rc2])
            cB = pr.dve(lambda e: e.tensor_copy(rbs[:, 512:1024], pb7[64:128, :]),
                        deps=[bB, cA])
            oA = pr.dve(lambda e, hp=hp, csl=csl: e.tensor_tensor(
                out_all[hp][0:64, csl], rbs[:, 0:512], pb4[0:64, :], ALU.mult), deps=[cA])
            oB = pr.dve(lambda e, hp=hp, csl=csl: e.tensor_tensor(
                out_all[hp][64:128, csl], rbs[:, 512:1024], pb5[0:64, :], ALU.mult),
                deps=[cB, oA])
            out_done[hp][tc_] = oB
            avA_last = oB
            return oB
        for tc in range(NTC):
            csl = slice(tc * 512, (tc + 1) * 512)
            exp_prev = [avA_last if carry_exps[0] is None else carry_exps[0],
                        avA_last if carry_exps[1] is None else carry_exps[1]]
            war_prev = [avA_last if c is None else c for c in carry_avs]
            exps = [None] * NTT
            avs = [None] * NTT
            def em_scores(j, dep_extra=(), csl=csl, exp_prev=exp_prev, tc=tc):
                par = j % 2
                sA = pbig[:, par * 1024:par * 1024 + 512]
                sB = pbig[:, par * 1024 + 512:par * 1024 + 1024]
                jsl = slice(j * 128, (j + 1) * 128)
                mA = pr.pe(lambda e, jsl=jsl, sA=sA, csl=csl: e.matmul(
                    sA, kp[0:64, jsl], qp[0:64, csl], start=True, stop=True),
                    deps=[qev[tc], kev[tc], cb4, exp_prev[par]] + list(dep_extra))
                mB = pr.pe(lambda e, jsl=jsl, sB=sB, csl=csl: e.matmul(
                    sB, kp[64:128, jsl], qp[64:128, csl], start=True, stop=True), deps=())
                return mA, mB
            def em_exp(j, mB, exps=exps, avs=avs, exp_prev=exp_prev, war_prev=war_prev):
                par = j % 2
                par4 = j % 4
                war = avs[j - 4][1] if j >= 4 else war_prev[par4]
                ex = pr.act(lambda e, par=par, par4=par4: e.activation(
                    prb[:, par4 * 1024:(par4 + 1) * 1024], pbig[:, par * 1024:(par + 1) * 1024],
                    AF.Exp, bias=negc[:, 0:1]), deps=[mB, war])
                exps[j] = ex
                exp_prev[par] = ex
                return ex
            def em_av(j, exps=exps, avs=avs):
                par4 = j % 4
                aA = pr.pe(lambda e, j=j, par4=par4: e.matmul(
                    pb4[0:65, :], vt[:, j * 130:j * 130 + 65],
                    prb[:, par4 * 1024:par4 * 1024 + 512],
                    start=(j == 0), stop=(j == NTT - 1)),
                    deps=[exps[j]] + ([avA_last] if j == 0 else []) + vtr_done[-1:])
                aB = pr.pe(lambda e, j=j, par4=par4: e.matmul(
                    pb5[0:65, :], vt[:, j * 130 + 65:j * 130 + 130],
                    prb[:, par4 * 1024 + 512:par4 * 1024 + 1024],
                    start=(j == 0), stop=(j == NTT - 1)), deps=())
                avs[j] = (aA, aB)
                return aA, aB
            for j in range(4):
                mA, mB = em_scores(j)
                em_exp(j, mB)
            if pending_evac is not None:
                emit_evac(pending_evac)
                pending_evac = None
            em_av(0)
            em_av(1)
            em_av(2)
            for j in range(4, NTT):
                mA, mB = em_scores(j)
                em_exp(j, mB)
                aA, aB = em_av(j - 1)
            aA, aB = em_av(NTT - 1)
            carry_exps = (exps[NTT - 2], exps[NTT - 1])
            carry_avs = (avs[NTT - 4][1], avs[NTT - 3][1], avs[NTT - 2][1], avs[NTT - 1][1])
            pending_evac = (aA, aB, csl, tc)
        oB_l = emit_evac(pending_evac)
        pending_evac = None
        prev_pair_ops = [oB_l, carry_avs[-2], carry_avs[-1], carry_exps[0], carry_exps[1]]
        pr.close(f"attn_p{hp}")

    wg_sb = sb("wg_sb", (128, 8192), BF16, at=SCR)
    gbuf = sb("gbuf", (128, 4096), BF16, at=SCR + 16 * 1024)
    lwg = pr.dmag(lambda e: e.dma_start(wg_sb[:].rearrange("p (g c) -> p g c", c=1024), dt["wgate"].rearrange("(g p) c -> p g c", p=128)),
                  deps=[qev[0]])
    og_done = [[None] * NTC for _ in range(NG)]
    prev_sig = [None] * NG
    for tc in range(NTC):
        csl = slice(tc * 512, (tc + 1) * 512)
        sig = [None] * NG
        for o_ in range(NG):
            pv = pbig[:, (o_ % 4) * 512:(o_ % 4) * 512 + 512]
            mms = []
            for g in range(NG):
                mms.append(pr.pe(lambda e, g=g, o_=o_, pv=pv, csl=csl: e.matmul(
                    pv, wg_sb[:, g * 1024 + o_ * 128:g * 1024 + (o_ + 1) * 128], out_all[g][:, csl],
                    start=(g == 0), stop=(g == NG - 1)),
                    deps=[lwg, out_done[g][tc]] + ([sig[o_ - 4]] if o_ >= 4
                                                    else [prev_sig[o_ + 4]])))
            sig[o_] = pr.act(lambda e, o_=o_, pv=pv: e.activation(
                gbuf[:, o_ * 512:(o_ + 1) * 512], pv, AF.Sigmoid, bias=ppc(P_GATEB)[:, o_:o_ + 1]),
                deps=[mms[-1], og_done[o_][tc - 1] if tc else None])
        for o_ in range(NG):
            og_done[o_][tc] = pr.dve(lambda e, o_=o_, csl=csl: e.tensor_tensor(
                out_all[o_][:, csl], out_all[o_][:, csl], gbuf[:, o_ * 512:(o_ + 1) * 512], ALU.mult),
                deps=[sig[NG - 1], og_done[o_][tc - 1] if tc else None])
        prev_sig = sig
    pr.close("gate")

    wo_sb = sb("wo_sb", (128, 8192), BF16, at=SCR)
    nbch = sb("nbch", (128, 2048), F32, at=SCR + 40 * 1024)
    z2sb2 = [sb(f"z2sb{i}", (128, 1024), F32, at=SCR + (16 if i == 0 else 32) * 1024)
             for i in range(2)]
    sqn = sb("sqn", (128, 1024), F32, at=SCR + 20 * 1024)
    x2ff = sb("x2ff", (128, 1024), F32, at=SCR + 24 * 1024)
    x1ts = sb("x1ts", (128, 1024), BF16, at=SCR + 36 * 1024)
    lgs = sb("lgs", (128, 8), F32, at=SCR + 30 * 1024)
    lgs2 = sb("lgs2", (128, 8), F32, at=SCR + 30 * 1024 + 512)
    lgs2b = sb("lgs2b", (128, 8), BF16, at=SCR + 30 * 1024 + 768)
    tkw = sb("tkw", (128, 16), F32, at=SCR + 30 * 1024 + 64)
    mgwf = sb("mgwf", (128, 64), F32, at=SCR + 31 * 1024)
    mwr = sb("mwr", (8, L), BF16, at=SCR + 48 * 1024)
    w1p_all = sb("w1p_all", (128, 32), F32, at=SCR + 56 * 1024)
    b2b = sb("b2b", (8, 1024), BF16, at=SCR + 56 * 1024 + 256)
    msk_all = sb("msk_all", (128, 8 * NTT), F32, at=SCR + 60 * 1024)
    x2fm = [sb(f"x2fm{g}", (128, L), BF16, at=Z_OFF + g * L * 4) for g in range(NG)]

    lwo = pr.dmag(lambda e: e.dma_start(wo_sb[:].rearrange("p (g c) -> p g c", c=1024), dt["wout"].rearrange("(g p) c -> p g c", p=128)),
                  deps=[lwg])
    lnb = pr.dma(lambda e: e.dma_start(nbch[:], dt["nrep"][:, 0:2 * D]))
    lmg = pr.dma(lambda e: e.dma_start(mgwf[:].rearrange("p (g c) -> p g c", c=8), dt["mgw"].rearrange("(g p) c -> p g c", p=128)))
    lb2 = pr.dmag(lambda e: e.dma_start(b2b[:], dt["b2"][:]))
    rows_d = pr.dve(lambda e: e.memset(tkw[:, 0:16], 0.0))
    prev_tt = [rows_d]
    x1s_dma = None
    c4_dbg_dmas = []
    x2_done = [[None] * NTT for _ in range(NG)]
    msk_done = [None] * NTT
    st = {"z2b": [None] * NTT, "cpt": [None] * NTT, "ap3": [None] * NTT,
          "ev2": [None] * NTT, "lg1": [None] * NTT, "mcp": [None] * NTT,
          "tps2": [None] * NTT, "mmz": [None] * NTT, "sqr": []}

    def stage1pe(tt):
        tsl = slice(tt * 128, (tt + 1) * 128)
        mm_z = []
        for dc in range(2):
            for g in range(NG):
                dps = []
                if g == 0:
                    dps = [lwo, og_done[g][-1], st["z2b"][tt - 1] if tt else rows_d]
                mm_z.append(pr.pe(lambda e, g=g, dc=dc, tsl=tsl: e.matmul(
                    pbig[:, dc * 512:(dc + 1) * 512],
                    out_all[g][:, tsl], wo_sb[:, g * 1024 + dc * 512:g * 1024 + (dc + 1) * 512],
                    start=(g == 0), stop=(g == NG - 1)), deps=dps))
        tps = []
        for g in range(NG):
            tps.append(pr.pe(lambda e, g=g, tsl=tsl: e.transpose(
                pbt[:, g * 128:(g + 1) * 128], x1b[g][:, tsl], identb[:]),
                deps=[x1_fin, st["cpt"][tt - 1] if tt else rows_d]))
        x1_users.append(tps[0])
        st["tps"] = tps
        st["mmz"][tt] = mm_z

    def stage1dve(tt):
        nonlocal x1s_dma
        tsl = slice(tt * 128, (tt + 1) * 128)
        z2v = z2sb2[tt % 2]
        mm_z = st["mmz"][tt]
        cpt = pr.dve(lambda e: e.tensor_copy(x1ts[:], pbt[:]), deps=[st["tps"][-1], x1s_dma])
        x1s_dma = pr.dma(lambda e, tsl=tsl: e.dma_start(x1tok[tsl, :], x1ts[:]), deps=[cpt])
        war = st["tps2"][tt - 2][-1] if tt >= 2 else None
        z2a = pr.dve(lambda e, z2v=z2v: e.tensor_tensor(
            z2v[:, 0:512], x1ts[:, 0:512], pbig[:, 0:512], ALU.add),
            deps=[cpt, mm_z[7], war])
        z2b = pr.dve(lambda e, z2v=z2v: e.tensor_tensor(
            z2v[:, 512:1024], x1ts[:, 512:1024], pbig[:, 512:1024], ALU.add),
            deps=[cpt, mm_z[15], z2a])
        st["cpt"][tt] = cpt
        st["z2b"][tt] = z2b

    def stage2(tt):
        tsl = slice(tt * 128, (tt + 1) * 128)
        z2v = z2sb2[tt % 2]
        z2b = st["z2b"][tt]
        rd = pr.dve(lambda e, z2v=z2v: e.tensor_reduce(tkw[:, 0:1], z2v[:], AX, ALU.add),
                    deps=[z2b, st.get("ddtk")])
        sq = pr.act(lambda e, z2v=z2v: e.activation(sqn[:], z2v[:], AF.Square,
                                                    accum_out=tkw[:, 1:2]),
                    deps=[z2b, st["ap3"][tt - 1] if tt else None, st.get("ddtk")] + st["sqr"])
        st["sqr"] = []
        v1 = pr.dve(lambda e: e.tensor_scalar(tkw[:, 2:3], tkw[:, 0:1], 1.0 / D, None, ALU.mult),
                    deps=[rd])
        v2 = pr.dve(lambda e: e.tensor_scalar(tkw[:, 3:4], tkw[:, 1:2], 1.0 / D, None, ALU.mult),
                    deps=[sq, v1])
        v3 = pr.dve(lambda e: e.tensor_tensor(tkw[:, 4:5], tkw[:, 2:3], tkw[:, 2:3], ALU.mult),
                    deps=[v2])
        v4 = pr.dve(lambda e: e.tensor_tensor(tkw[:, 3:4], tkw[:, 3:4], tkw[:, 4:5], ALU.subtract),
                    deps=[v3])
        v5 = pr.act(lambda e: e.activation(tkw[:, 3:4], tkw[:, 3:4], AF.Ln, bias=cvals[:, 0:1]),
                    deps=[v4])
        v6 = pr.act(lambda e: e.activation(tkw[:, 3:4], tkw[:, 3:4], AF.Exp, scale=-0.5), deps=[v5])
        ap1 = pr.dve(lambda e, z2v=z2v: e.tensor_scalar(sqn[:], z2v[:], tkw[:, 2:3], tkw[:, 3:4],
                                                        ALU.subtract, ALU.mult), deps=[v6, sq])
        ap2 = pr.dve(lambda e: e.tensor_tensor(sqn[:], sqn[:], nbch[:, 0:D], ALU.mult),
                     deps=[ap1, lnb])
        ap3 = pr.dve(lambda e, z2v=z2v: e.tensor_tensor(z2v[:], sqn[:], nbch[:, D:2 * D], ALU.add),
                     deps=[ap2])
        st["ap3"][tt] = ap3
        tps2 = []
        for g in range(NG):
            tps2.append(pr.pe(lambda e, g=g, z2v=z2v: e.transpose(
                pbig[:, 1024 + g * 128:1024 + (g + 1) * 128], z2v[:, g * 128:(g + 1) * 128],
                identf[:]), deps=[ap3, ld["identf"]] +
                ([tps2[-1]] if g else [st["ev2"][tt - 1] if tt else None])))
        st["tps2"][tt] = tps2
        ev2 = pr.dve(lambda e: e.tensor_copy(x2ff[:], pbig[:, 1024:2048]), deps=[tps2[-1]])
        st["ev2"][tt] = ev2
        cp2 = []
        for g in range(NG):
            cp2.append(pr.dve(lambda e, g=g, tsl=tsl: e.tensor_copy(
                x2fm[g][:, tsl], x2ff[:, g * 128:(g + 1) * 128]),
                deps=[ev2]))
            x2_done[g][tt] = cp2[-1]
        extra_lmm_dep = []
        if "dxf" in dbg and tt == 1:
            ddx = pr.dma(lambda e: e.dma_start(dbg["dxf"][:], x2ff[:]), deps=[ev2])
            extra_lmm_dep = [ddx]
            c4_dbg_dmas.append(ddx)
        lmm = []
        for g in range(NG):
            lmm.append(pr.pe(lambda e, g=g: e.matmul(
                pb7[:, 256:264], x2ff[:, g * 128:(g + 1) * 128], mgwf[:, g * 8:(g + 1) * 8],
                start=(g == 0), stop=(g == NG - 1)),
                deps=[ev2, lmg, tps2[-1], st["lg1"][tt - 1] if tt else None] + extra_lmm_dep))
        lg1 = pr.dve(lambda e: e.tensor_tensor(lgs[:], mgb[:], pb7[:, 256:264], ALU.add),
                     deps=[lmm[-1], ld["mgb"], st.get("ddlg")])
        st["lg1"][tt] = lg1
        m1 = pr.dve(lambda e: e.tensor_reduce(tkw[:, 5:6], lgs[:], AX, ALU.max), deps=[lg1])
        m1n = pr.dve(lambda e: e.tensor_scalar(tkw[:, 6:7], tkw[:, 5:6], -1.0, None, ALU.mult),
                     deps=[m1])
        eq = pr.dve(lambda e: e.tensor_scalar(lgs2[:], lgs[:], tkw[:, 5:6],
                                              None, ALU.is_equal), deps=[m1])
        mkd = pr.dve(lambda e: e.scalar_tensor_tensor(lgs2[:], lgs2[:], -1e30, lgs[:],
                                                      ALU.mult, ALU.add), deps=[eq])
        m2 = pr.dve(lambda e: e.tensor_reduce(tkw[:, 7:8], lgs2[:], AX, ALU.max), deps=[mkd])
        ge2 = pr.dve(lambda e: e.tensor_scalar(lgs2[:], lgs[:], tkw[:, 7:8], None, ALU.is_ge),
                     deps=[m2])
        cand = pr.dve(lambda e: e.tensor_tensor(lgs2[:], lgs2[:], iota8[:], ALU.mult),
                      deps=[ge2, ld["iota8"]])
        cho = pr.dve(lambda e: e.tensor_reduce(tkw[:, 8:9], lgs2[:], AX, ALU.max), deps=[cand])
        e2 = pr.act(lambda e: e.activation(tkw[:, 9:10], tkw[:, 7:8], AF.Exp, bias=tkw[:, 6:7]),
                    deps=[m2, m1n])
        s_ = pr.dve(lambda e: e.tensor_scalar(tkw[:, 10:11], tkw[:, 9:10], 1.0, None, ALU.add),
                    deps=[e2])
        rcp = pr.dve(lambda e: e.reciprocal(tkw[:, 11:12], tkw[:, 10:11]), deps=[s_])
        p2_ = pr.dve(lambda e: e.tensor_tensor(tkw[:, 12:13], tkw[:, 9:10], tkw[:, 11:12], ALU.mult),
                     deps=[rcp])
        wsm = pr.dve(lambda e: e.tensor_tensor(tkw[:, 13:14], tkw[:, 11:12], tkw[:, 12:13], ALU.add),
                     deps=[p2_])
        wp1 = pr.dve(lambda e, tt=tt: e.tensor_scalar(w1p_all[:, tt:tt + 1], tkw[:, 13:14], 1.0, None,
                                                      ALU.add), deps=[wsm])
        st["wp1"] = wp1
        msk = pr.dve(lambda e: e.tensor_scalar(lgs2[:], iota8[:], tkw[:, 8:9], None, ALU.is_equal),
                     deps=[cho, cand])
        mw1 = pr.dve(lambda e: e.tensor_scalar(lgs2[:], lgs2[:], tkw[:, 13:14], None, ALU.mult),
                     deps=[msk, wsm])
        mw2 = pr.dve(lambda e: e.tensor_tensor(lgs2b[:], lgs2[:], rscr[:], ALU.mult),
                     deps=[mw1, ld["rsc"]])
        msA = pr.dve(lambda e, tt=tt: e.tensor_tensor(msk_all[:, tt * 8:(tt + 1) * 8],
                                                      lgs2[:], rscr[:], ALU.mult),
                     deps=[mw1, ld["rsc"]])
        mtp = pr.pe(lambda e: e.matmul(pb7[0:8, 0:128], lgs2b[:], identb[:],
                                       start=True, stop=True),
                    deps=[mw2, ld["identb"], st["mcp"][tt - 1] if tt else None])
        mcp = pr.dve(lambda e, tsl=tsl: e.tensor_copy(mwr[:, tsl], pb7[0:8, 0:128]), deps=[mtp])
        st["mcp"][tt] = mcp
        msk_done[tt] = mcp
        if "dxf" in dbg and tt == 1:
            ddp1 = pr.dve(lambda e: e.tensor_copy(sqn[:, 0:8], pb7[:, 256:264]),
                          deps=[lmm[-1], ap1, sq])
            ddp2 = pr.dma(lambda e: e.dma_start(dbg["dps"][:], sqn[:, 0:8]), deps=[ddp1])
            c4_dbg_dmas.append(ddp2)
            st["sqr"] += [ddp1, ddp2]
        if "dtk" in dbg:
            dd2 = pr.dma(lambda e, tt=tt: e.dma_start(dbg["dtk"][:, tt * 16:(tt + 1) * 16],
                                                      tkw[:]), deps=[wp1, mw2, mcp])
            dd4 = pr.dma(lambda e, tt=tt: e.dma_start(dbg["dlg"][:, tt * 8:(tt + 1) * 8],
                                                      lgs[:]), deps=[lg1, m1, mkd])
            c4_dbg_dmas.extend([dd2, dd4])
            st["ddtk"] = dd2
            st["ddlg"] = dd4

    stage1pe(0)
    stage1dve(0)
    for tt in range(1, NTT):
        stage1pe(tt)
        stage2(tt - 1)
        stage1dve(tt)
    stage2(NTT - 1)
    pr.close("c4")

    if "dx2" in dbg or "dmwr" in dbg or "doa" in dbg:
        pdmas = []
        for g in range(NG):
            if "dx2" in dbg:
                pdmas.append(pr.dmag(lambda e, g=g: e.dma_start(
                    dbg["dx2"][g * 128:(g + 1) * 128, :], x2fm[g][:]), deps=[x2_done[g][-1]]))
            if "doa" in dbg:
                pdmas.append(pr.dmag(lambda e, g=g: e.dma_start(
                    dbg["doa"][g * 128:(g + 1) * 128, :], out_all[g][:]), deps=[og_done[g][-1]]))
        if "dmwr" in dbg:
            pdmas.append(pr.dmag(lambda e: e.dma_start(dbg["dmwr"][:], mwr[:]), deps=[msk_done[-1]]))
        pr.close("dbg_c")
        moe_wait = pdmas + c4_dbg_dmas[-20:] + [x1s_dma]
    else:
        moe_wait = c4_dbg_dmas[-20:] + [x1s_dma]

    TCH = 512
    NTC2 = L // TCH
    FD = 2048
    if L == 4096:
        w1h = [sb(f"w1h{g}", (128, FD), BF16, at=Z_OFF + g * L * 4 + L * 2) for g in range(NG)]
        w2h = [sb(f"w2h{f}", (128, D), BF16,
                  at=Z_OFF + (f // 2) * L * 4 + L * 2 + 4096 + (f % 2) * 2048) for f in range(16)]
    else:
        w1h = [sb(f"w1h{g}", (128, FD), BF16, at=SCR + 72 * 1024 + g * 4096) for g in range(NG)]
        w2h = [sb(f"w2h{f}", (128, D), BF16, at=SCR + 104 * 1024 + f * 2048) for f in range(16)]
    hhB = sb("hhB", (128, 16 * TCH), BF16, at=SCR)
    hgb = sb("hgb", (128, 16 * TCH), BF16, at=SCR + 16 * 1024)
    ysb = sb("ysb", (128, 1024), F32, at=SCR + 32 * 1024)
    mrsb = sb("mrsb", (128, 2 * TCH), F32, at=SCR + 36 * 1024)
    wrk2 = sb("wrk2", (128, 2 * TCH), F32, at=SCR + 40 * 1024)
    sq2m = sb("sq2m", (128, 2 * TCH), BF16, at=SCR + 44 * 1024)
    bcols = sb("bcols", (128, 48), F32, at=SCR + 46 * 1024)
    rm = sb("rm", (1, 3 * TCH), F32, at=10 * 1024)
    mrsb_user = None

    w1_users = [None] * NG
    w2_users = [None] * 16
    b_users = None
    yacc_done = [[None, None] for _ in range(NTT)]
    ysb_user = [None, None]
    prev_stats = [None, None]
    wrk_gl = [None, None]
    sq_mm = [None, None]
    prev_a1 = [None] * 16
    prev_py = None
    prev_w2tail = None
    pending = None
    step = 0

    def emit_w2(pend):
        nonlocal prev_py, prev_w2tail, ysb_user
        e_, tch, ap_last, lw2e_ = pend
        for tt2 in range(TCH // 128):
            ttg = tch * (TCH // 128) + tt2
            mm2s = []
            for dc in range(2):
                for ft in range(16):
                    dps = []
                    if tch == 0 and tt2 == 0 and dc == 0:
                        dps.append(lw2e_[ft])
                    if ft == 0:
                        dps += [ap_last, prev_py]
                    mm2s.append(pr.pe(lambda e, ft=ft, dc=dc, tt2=tt2: e.matmul(
                        pbig[:, 1024 + dc * 512:1024 + (dc + 1) * 512],
                        hgb[:, ft * TCH + tt2 * 128:ft * TCH + (tt2 + 1) * 128],
                        w2h[ft][:, dc * 512:(dc + 1) * 512],
                        start=(ft == 0), stop=(ft == 15)), deps=dps))
                    w2_users[ft] = mm2s[-1]
            dys = []
            for h in range(2):
                ev2 = pr.dve(lambda e, e_=e_, ttg=ttg, h=h: e.tensor_scalar(
                    ysb[:, h * 512:(h + 1) * 512], pbig[:, 1024 + h * 512:1536 + h * 512],
                    msk_all[:, ttg * 8 + e_:ttg * 8 + e_ + 1], None, ALU.mult),
                    deps=[mm2s[15 if h == 0 else 31], ysb_user[h]])
                dy = pr.dmag(lambda e, e_=e_, ttg=ttg, h=h: e.dma_start(
                    yacc[ttg * 128:(ttg + 1) * 128, h * 512:(h + 1) * 512],
                    ysb[:, h * 512:(h + 1) * 512],
                    accum_op=(ALU.bypass if e_ == 0 else ALU.add)),
                    deps=[ev2, yacc_done[ttg][h]])
                dys.append(dy)
                ysb_user[h] = dy
                prev_py = ev2
            yacc_done[ttg] = dys
            prev_w2tail = mm2s[-1]

    lw2e = None
    deferred = None

    def apply_step(ctx, ft):
        nonlocal mrsb_user, b_users
        wv = wrk2[:, (ft % 2) * TCH:(ft % 2) * TCH + TCH]
        a1 = pr.dve(lambda e, wv=wv, ft=ft: e.tensor_tensor(
            wv, hhB[:, ft * TCH:(ft + 1) * TCH], mrsb[:, 0:TCH], ALU.subtract),
            deps=[ctx["mbev"], wrk_gl[ft % 2], ctx["evB"][ft]])
        prev_a1[ft] = a1
        a2 = pr.dve(lambda e, wv=wv: e.tensor_tensor(wv, wv, mrsb[:, TCH:2 * TCH],
                                                     ALU.mult), deps=[ctx["rbev"], a1])
        mrsb_user = a2
        a3 = pr.dve(lambda e, wv=wv, ft=ft: e.tensor_scalar(
            wv, wv, bcols[:, 16 + ft:17 + ft], bcols[:, 32 + ft:33 + ft],
            ALU.mult, ALU.add), deps=[a2, ctx["lgc"], ctx["lbc2"]])
        gl = pr.act(lambda e, wv=wv, ft=ft: e.activation(
            hgb[:, ft * TCH:(ft + 1) * TCH], wv, AF.Gelu), deps=[a3, prev_w2tail])
        wrk_gl[ft % 2] = gl
        if ft == 15:
            ctx["pend"][2] = gl
            b_users = a3

    for e_ in range(NE):
        for tch in range(NTC2):
            if tch == 0:
                if deferred is not None:
                    for ft in range(16):
                        apply_step(deferred, ft)
                    deferred = None
                lw1e = [pr.dmag(lambda e, e_=e_, g=g: e.dma_start(
                    w1h[g][:], dt["w1"][e_, g * 128:(g + 1) * 128, :]), deps=[w1_users[g]])
                    for g in range(NG)]
                if pending is not None:
                    emit_w2(pending)
                    pending = None
                lw2e = [pr.dmag(lambda e, e_=e_, f=f: e.dma_start(
                    w2h[f][:], dt["w2"][e_, f * 128:(f + 1) * 128, :]), deps=[w2_users[f]])
                    for f in range(16)]
                lbc = pr.dma(lambda e, e_=e_: e.dma_start(bcols[:, 0:16], dt["b1p"][e_]),
                             deps=[b_users])
                lgc = pr.dma(lambda e, e_=e_: e.dma_start(bcols[:, 16:32], dt["lgp"][e_]),
                             deps=[b_users])
                lbc2 = pr.dma(lambda e, e_=e_: e.dma_start(bcols[:, 32:48], dt["lbp"][e_]),
                              deps=[b_users])
            csl = slice(tch * TCH, (tch + 1) * TCH)
            evB = [None] * 16
            sqe_l = [None] * 16
            nmm1 = nmm2 = None
            ctx = deferred
            deferred = None
            for ft in range(16):
                pv = pbig[:, (ft % 2) * TCH:(ft % 2) * TCH + TCH]
                sqs = sq2m[:, (ft % 2) * TCH:(ft % 2) * TCH + TCH]
                mms = []
                for g in range(NG):
                    dps = []
                    if ft == 0 and tch == 0:
                        dps.append(lw1e[g])
                    if g == 0:
                        dps += [x2_done[g][-1]] + ([evB[ft - 2], sqe_l[ft - 2]] if ft >= 2 else [])
                    mms.append(pr.pe(lambda e, g=g, ft=ft, pv=pv, csl=csl: e.matmul(
                        pv, w1h[g][:, ft * 128:(ft + 1) * 128], x2fm[g][:, csl],
                        start=(g == 0), stop=(g == NG - 1)), deps=dps))
                    w1_users[g] = mms[-1]
                if ctx is not None:
                    apply_step(ctx, ft)
                evB[ft] = pr.act(lambda e, ft=ft, pv=pv: e.activation(
                    hhB[:, ft * TCH:(ft + 1) * TCH], pv, AF.Identity, bias=bcols[:, ft:ft + 1]),
                    deps=[mms[-1], lbc, prev_a1[ft]])
                sqe_l[ft] = pr.act(lambda e, ft=ft, pv=pv, sqs=sqs: e.activation(
                    sqs, pv, AF.Square, bias=bcols[:, ft:ft + 1]),
                    deps=[mms[-1], lbc, sq_mm[ft % 2]])
                nmm1 = pr.pe(lambda e, ft=ft: e.matmul(
                    pb4[0:1, 0:TCH], onesc_b[:], hhB[:, ft * TCH:(ft + 1) * TCH],
                    start=(ft == 0), stop=(ft == 15)),
                    deps=[evB[ft]] + ([prev_stats[0]] if ft == 0 else []))
                nmm2 = pr.pe(lambda e, ft=ft, sqs=sqs: e.matmul(
                    pb5[0:1, 0:TCH], onesc_b[:], sqs,
                    start=(ft == 0), stop=(ft == 15)),
                    deps=[sqe_l[ft]] + ([prev_stats[1]] if ft == 0 else []))
                sq_mm[ft % 2] = nmm2
            s0 = pr.dve(lambda e: e.tensor_scalar(rm[0:1, 0:TCH], pb4[0:1, 0:TCH], 1.0 / FD, None,
                                                  ALU.mult), deps=[nmm1])
            s1 = pr.dve(lambda e: e.tensor_scalar(rm[0:1, TCH:2 * TCH], pb5[0:1, 0:TCH], 1.0 / FD,
                                                  None, ALU.mult), deps=[nmm2, s0])
            s2 = pr.dve(lambda e: e.tensor_tensor(rm[0:1, 2 * TCH:3 * TCH], rm[0:1, 0:TCH],
                                                  rm[0:1, 0:TCH], ALU.mult), deps=[s1])
            s3 = pr.dve(lambda e: e.tensor_tensor(rm[0:1, TCH:2 * TCH], rm[0:1, TCH:2 * TCH],
                                                  rm[0:1, 2 * TCH:3 * TCH], ALU.subtract), deps=[s2])
            s4 = pr.act(lambda e: e.activation(rm[0:1, TCH:2 * TCH], rm[0:1, TCH:2 * TCH], AF.Ln,
                                               bias=cvals[0:1, 0:1]), deps=[s3])
            s5 = pr.act(lambda e: e.activation(rm[0:1, TCH:2 * TCH], rm[0:1, TCH:2 * TCH], AF.Exp,
                                               scale=-0.5), deps=[s4])
            if pending is not None:
                emit_w2(pending)
                pending = None
            bm = None
            for hf in range(TCH // 256):
                bm = pr.pe(lambda e, hf=hf: e.matmul(
                    pb4[:, hf * 256:(hf + 1) * 256], onesr_f[:],
                    rm[0:1, hf * 256:hf * 256 + 256], start=True, stop=True),
                    deps=[s2] if hf == 0 else [])
            br = None
            for hf in range(TCH // 256):
                br = pr.pe(lambda e, hf=hf: e.matmul(
                    pb5[:, hf * 256:(hf + 1) * 256], onesr_f[:],
                    rm[0:1, TCH + hf * 256:TCH + hf * 256 + 256], start=True, stop=True),
                    deps=[s5, bm] if hf == 0 else [])
            mbev = pr.dve(lambda e: e.tensor_copy(mrsb[:, 0:TCH], pb4[:, 0:TCH]),
                          deps=[bm, mrsb_user] + (moe_wait if step == 0 else []))
            rbev = pr.dve(lambda e: e.tensor_copy(mrsb[:, TCH:2 * TCH], pb5[:, 0:TCH]),
                          deps=[br, mbev])
            prev_stats = [mbev, rbev]
            mrsb_user = None
            pend = [e_, tch, None, lw2e]
            deferred = {"mbev": mbev, "rbev": rbev, "lgc": lgc, "lbc2": lbc2,
                        "evB": evB, "pend": pend}
            pending = pend
            step += 1
    for ft in range(16):
        apply_step(deferred, ft)
    deferred = None
    emit_w2(pending)
    pending = None
    pr.close("moe")

    yacc_sb2 = [sb(f"yacc_sb{i}", (128, 1024), F32, at=SCR + (0 if i == 0 else 22 * 1024))
                for i in range(2)]
    z32 = [sb(f"z3{i}", (128, 1024), F32, at=SCR + (4 if i == 0 else 28) * 1024)
           for i in range(2)]
    sqe_ = sb("sqe", (128, 1024), F32, at=SCR + 8 * 1024)
    x2ts = sb("x2ts", (128, 1024), BF16, at=SCR + 12 * 1024)
    tke = sb("tke", (128, 8), F32, at=SCR + 14 * 1024)
    x1e2 = [sb(f"x1e{i}", (128, 1024), BF16, at=SCR + (16 if i == 0 else 26) * 1024)
            for i in range(2)]
    obuf = sb("obuf", (128, 1024), F32, at=SCR + 18 * 1024)
    lnb2 = pr.dma(lambda e: e.dma_start(nbch[:], dt["nrep"][:, 2 * D:4 * D]), deps=[])
    fin = []
    se = {"z3a": [None] * NTT, "z3c": [None] * NTT, "xcp": [None] * NTT,
          "ly": [None] * NTT, "lx1": [None] * NTT, "f1": [None] * NTT,
          "f3": [None] * NTT, "e2": [None] * NTT, "fa": [None] * NTT, "od": [None] * NTT}

    def stage1epe(tt):
        tsl = slice(tt * 128, (tt + 1) * 128)
        par = tt % 2
        ysb_e = yacc_sb2[par]
        ly = pr.dma(lambda e, tsl=tsl, ysb_e=ysb_e: e.dma_start(ysb_e[:], yacc[tsl, :]),
                    deps=yacc_done[tt] + [se["z3a"][tt - 2] if tt >= 2 else None])
        lx1 = pr.dma(lambda e, tsl=tsl, par=par: e.dma_start(x1e2[par][:], x1tok[tsl, :]),
                     deps=[x1s_dma, se["f1"][tt - 2] if tt >= 2 else None])
        b2m = [pr.pe(lambda e, dc=dc, tsl=tsl: e.matmul(
            pbig[:, dc * 512:(dc + 1) * 512], mwr[:, tsl], b2b[:, dc * 512:(dc + 1) * 512],
            start=True, stop=True),
            deps=[msk_done[tt], lb2, se["z3c"][tt - 1] if tt else None]) for dc in range(2)]
        tps = []
        for g in range(NG):
            tps.append(pr.pe(lambda e, g=g, tsl=tsl: e.transpose(
                pbt[:, g * 128:(g + 1) * 128], x2fm[g][:, tsl], identb[:]),
                deps=[x2_done[g][-1], se["xcp"][tt - 1] if tt else None]))
        se["ly"][tt] = ly
        se["lx1"][tt] = lx1
        se["tps"] = tps
        se["b2m"] = b2m

    def stage1edve(tt):
        par = tt % 2
        ysb_e = yacc_sb2[par]
        z3v = z32[par]
        b2m = se["b2m"]
        xcp = pr.dve(lambda e: e.tensor_copy(x2ts[:], pbt[:]), deps=[se["tps"][-1]])
        z3a = pr.dve(lambda e, tt=tt, z3v=z3v, ysb_e=ysb_e: e.scalar_tensor_tensor(
            z3v[:], x2ts[:], w1p_all[:, tt:tt + 1], ysb_e[:], ALU.mult, ALU.add),
            deps=[xcp, se["ly"][tt], st["wp1"], se["f3"][tt - 2] if tt >= 2 else None])
        z3b = pr.dve(lambda e, z3v=z3v: e.tensor_tensor(
            z3v[:, 0:512], z3v[:, 0:512], pbig[:, 0:512], ALU.add), deps=[z3a, b2m[0]])
        z3c = pr.dve(lambda e, z3v=z3v: e.tensor_tensor(
            z3v[:, 512:1024], z3v[:, 512:1024], pbig[:, 512:1024], ALU.add),
            deps=[z3b, b2m[1]])
        se["xcp"][tt] = xcp
        se["z3a"][tt] = z3a
        se["z3c"][tt] = z3c

    def stage2e(tt):
        tsl = slice(tt * 128, (tt + 1) * 128)
        par = tt % 2
        z3v = z32[par]
        z3c = se["z3c"][tt]
        rd = pr.dve(lambda e, z3v=z3v: e.tensor_reduce(tke[:, 0:1], z3v[:], AX, ALU.add),
                    deps=[z3c])
        sq = pr.act(lambda e, z3v=z3v: e.activation(sqe_[:], z3v[:], AF.Square,
                                                    accum_out=tke[:, 1:2]),
                    deps=[z3c, se["f1"][tt - 1] if tt else None])
        v1 = pr.dve(lambda e: e.tensor_scalar(tke[:, 2:3], tke[:, 0:1], 1.0 / D, None, ALU.mult),
                    deps=[rd])
        v2 = pr.dve(lambda e: e.tensor_scalar(tke[:, 3:4], tke[:, 1:2], 1.0 / D, None, ALU.mult),
                    deps=[sq, v1])
        v3 = pr.dve(lambda e: e.tensor_tensor(tke[:, 4:5], tke[:, 2:3], tke[:, 2:3], ALU.mult),
                    deps=[v2])
        v4 = pr.dve(lambda e: e.tensor_tensor(tke[:, 3:4], tke[:, 3:4], tke[:, 4:5], ALU.subtract),
                    deps=[v3])
        v5 = pr.act(lambda e: e.activation(tke[:, 3:4], tke[:, 3:4], AF.Ln, bias=cvals[:, 0:1]),
                    deps=[v4])
        v6 = pr.act(lambda e: e.activation(tke[:, 3:4], tke[:, 3:4], AF.Exp, scale=-0.5), deps=[v5])
        e1 = pr.dve(lambda e, z3v=z3v: e.tensor_scalar(sqe_[:], z3v[:], tke[:, 2:3], tke[:, 3:4],
                                                       ALU.subtract, ALU.mult), deps=[v6, sq])
        e2_ = pr.dve(lambda e: e.tensor_tensor(sqe_[:], sqe_[:], nbch[:, 0:D], ALU.mult),
                     deps=[e1, lnb2])
        e3 = pr.dve(lambda e, z3v=z3v: e.tensor_tensor(z3v[:], sqe_[:], nbch[:, D:2 * D], ALU.add),
                    deps=[e2_])
        se["e2"][tt] = e2_
        f1 = pr.dve(lambda e, z3v=z3v, par=par: e.tensor_tensor(z3v[:], z3v[:], x1e2[par][:],
                                                                ALU.add),
                    deps=[e3, se["lx1"][tt]])
        se["f1"][tt] = f1
        f2 = pr.dve(lambda e, z3v=z3v: e.tensor_reduce(tke[:, 5:6], z3v[:], AX, ALU.add),
                    deps=[f1])
        f3 = pr.act(lambda e, z3v=z3v: e.activation(sqe_[:], z3v[:], AF.Square,
                                                    accum_out=tke[:, 6:7]), deps=[f1, e2_])
        se["f3"][tt] = f3
        f4 = pr.dve(lambda e: e.tensor_scalar(tke[:, 5:6], tke[:, 5:6], 1.0 / D, None, ALU.mult),
                    deps=[f2])
        f5 = pr.dve(lambda e: e.tensor_scalar(tke[:, 6:7], tke[:, 6:7], 1.0 / D, None, ALU.mult),
                    deps=[f3, f4])
        f6 = pr.dve(lambda e: e.tensor_tensor(tke[:, 7:8], tke[:, 5:6], tke[:, 5:6], ALU.mult),
                    deps=[f5])
        f7 = pr.dve(lambda e: e.tensor_tensor(tke[:, 6:7], tke[:, 6:7], tke[:, 7:8], ALU.subtract),
                    deps=[f6])
        f8 = pr.act(lambda e: e.activation(tke[:, 6:7], tke[:, 6:7], AF.Ln, bias=cvals[:, 0:1]),
                    deps=[f7])
        f9 = pr.act(lambda e: e.activation(tke[:, 6:7], tke[:, 6:7], AF.Exp, scale=-0.5), deps=[f8])
        fa = pr.dve(lambda e, z3v=z3v: e.tensor_scalar(obuf[:], z3v[:], tke[:, 5:6], tke[:, 6:7],
                                                       ALU.subtract, ALU.mult),
                    deps=[f9, f3, se["od"][tt - 1] if tt else None])
        od = pr.dma(lambda e, tsl=tsl: e.dma_start(out[tsl, :], obuf[:]), deps=[fa])
        se["fa"][tt] = fa
        se["od"][tt] = od
        fin.append(od)

    stage1epe(0)
    stage1edve(0)
    for tt in range(1, NTT):
        stage1epe(tt)
        stage2e(tt - 1)
        stage1edve(tt)
    stage2e(NTT - 1)
    pr.close("e")
    pr.final_wait(fin)

    return dict(dt=dt, out=out, dbg=dbg)



def host_pack(inputs, b, L=4096):
    ii = inputs
    import ml_dtypes
    x = np.asarray(ii["L0_attn_x"][b], dtype=np.float32)
    d = {}
    d["xt"] = np.ascontiguousarray(x[:L].T)
    pp = np.zeros((128, 64), np.float32)
    for i, nm in enumerate(["L0_attn_phazor_init_re", "L0_attn_phazor_init_im",
                            "L0_attn_phazor_angle", "L0_attn_ln_minus_ln_phazor_abs",
                            "L0_attn_last_conv_re", "L0_attn_last_conv_im",
                            "L0_attn_ln_phazor_scale"]):
        pp[:, i * 8:(i + 1) * 8] = np.asarray(ii[nm], np.float32).reshape(8, 128).T
    pp[:, 56:64] = np.asarray(ii["L0_ffn_gate_b"], np.float32).reshape(8, 128).T
    d["pp"] = pp
    nrow = np.concatenate([np.asarray(ii[k], np.float32) for k in
                           ["L0_ffn_n1_g", "L0_ffn_n1_b", "L0_ffn_n2_g", "L0_ffn_n2_b"]])
    d["nrep"] = np.broadcast_to(nrow, (128, 4 * D)).copy()
    for nm, key in [("wq", "L0_ffn_q_w"), ("wk", "L0_ffn_k_w"), ("wv", "L0_ffn_v_w")]:
        w = np.asarray(ii[key], np.float32)
        d[nm] = np.ascontiguousarray(w.reshape(1024, 8, 128).transpose(1, 0, 2))
    d["wgate"] = np.asarray(ii["L0_ffn_gate_w"], np.float32)
    d["wout"] = np.asarray(ii["L0_ffn_out_w"], np.float32)
    sw = np.asarray(ii["L0_ffn_scale_w"], np.float32)
    swp = np.zeros((8, 128, 2), np.float32)
    for hp in range(8):
        swp[hp, 0:64, 0] = sw[2 * hp]
        swp[hp, 64:128, 1] = sw[2 * hp + 1]
    d["swp"] = swp
    d["mgw"] = np.asarray(ii["L0_ffn_moe_gate_w"], np.float32)
    d["mgb_rep"] = np.broadcast_to(np.asarray(ii["L0_ffn_moe_gate_b"], np.float32), (128, 8)).copy()
    d["rsc_rep"] = np.broadcast_to(np.asarray(ii["L0_ffn_res_scale"], np.float32), (128, 8)).copy()
    d["w1"] = np.asarray(ii["L0_ffn_w1"], np.float32)
    d["w2"] = np.asarray(ii["L0_ffn_w2"], np.float32)
    for nm, key in [("b1p", "L0_ffn_b1"), ("lgp", "L0_ffn_ln_g"), ("lbp", "L0_ffn_ln_b")]:
        w = np.asarray(ii[key], np.float32)
        d[nm] = np.ascontiguousarray(w.reshape(8, 16, 128).transpose(0, 2, 1))
    d["b2"] = np.asarray(ii["L0_ffn_b2"], np.float32)
    d["identb"] = np.eye(128, dtype=ml_dtypes.bfloat16)
    d["identf"] = np.eye(128, dtype=np.float32)
    d["iota8"] = np.broadcast_to(np.arange(8, dtype=np.float32), (128, 8)).copy()
    d["onesc_b"] = np.ones((128, 1), ml_dtypes.bfloat16)
    d["onesr_f"] = np.ones((1, 128), np.float32)
    bc2 = np.zeros((2, 128), np.float32)
    bc2[0, 0:64] = 1.0
    bc2[1, 64:128] = 1.0
    d["bc2"] = bc2
    so = np.zeros((128, 2), ml_dtypes.bfloat16)
    so[0:64, 0] = 1.0
    so[64:128, 1] = 1.0
    d["swones_b"] = so
    eoh = np.zeros((8, 8 * 128), ml_dtypes.bfloat16)
    for e in range(8):
        eoh[e, e * 128:(e + 1) * 128] = 1.0
    d["eoh"] = eoh
    return d


def kernel(**inputs):
    nc = bass.Bass("TRN2", target_bir_lowering=False)
    build(nc)
    nc.finalize()
    in_maps = [host_pack(inputs, b) for b in range(8)]
    res = run_bass_kernel_spmd(nc, in_maps, core_ids=list(range(8)))
    return np.stack([np.asarray(r["out"]) for r in res.results], axis=0)
